# version 01 (final)
# baseline (speedup 1.0000x reference)
import numpy as np
import ml_dtypes
from contextlib import ExitStack

import concourse.bass as bass
import concourse.mybir as mybir
from concourse.bass_utils import run_bass_kernel_spmd
from concourse.bass_core_rust import IndirectOffsetOnAxis


F32 = mybir.dt.float32
BF16 = mybir.dt.bfloat16
I32 = mybir.dt.int32
AF = mybir.ActivationFunctionType
ALU = mybir.AluOpType
AX = mybir.AxisListType

ENGINES = ("sync", "scalar", "vector", "tensor", "gpsimd")


class Sem:
    __slots__ = ("name", "h", "count")

    def __init__(self, name, h):
        self.name = name
        self.h = h
        self.count = 0


class KB:
    def __init__(self, nc: bass.Bass):
        self.nc = nc
        self.stack = ExitStack()
        self.streams = {e: [] for e in ENGINES}
        self.sems = {}
        self.chain = {}
        for e in ("scalar", "vector", "gpsimd", "tensor"):
            self.chain[e] = self.sem(f"ch_{e}")
        self.dma_pools = {}
        self.dma_rr = {}
        for e, n in (("sync", 10), ("gpsimd", 10), ("scalar", 6)):
            self.dma_pools[e] = [self.sem(f"dma_{e}_{i}") for i in range(n)]
            self.dma_rr[e] = 0
        self._finalized = False

    def sem(self, name):
        h = self.stack.enter_context(self.nc.semaphore(name))
        s = Sem(name, h)
        self.sems[name] = s
        return s

    def sb(self, name, shape, dtype):
        return self.stack.enter_context(self.nc.sbuf_tensor(name, list(shape), dtype))

    def ps(self, name, shape, dtype=F32):
        return self.stack.enter_context(self.nc.psum_tensor(name, list(shape), dtype))

    def dram(self, name, shape, dtype, kind="Internal"):
        return self.nc.dram_tensor(name, list(shape), dtype, kind=kind).ap()

    def _issue(self, eng_name, call, waits, incs):
        waits = [(s, v) for (s, v) in waits]

        def f(eng):
            for s, v in waits:
                eng.wait_ge(s.h, v)
            inst = call(eng) if call is not None else None
            if inst is not None:
                for s, n in incs:
                    inst = inst.then_inc(s.h, n)
        self.streams[eng_name].append(f)

    def op(self, eng_name, call, waits=(), extra_inc=()):
        ch = self.chain[eng_name]
        w = list(waits) + [(ch, ch.count)]
        ch.count += 1
        incs = [(ch, 1)] + list(extra_inc)
        self._issue(eng_name, call, w, incs)
        return (ch, ch.count)

    def v(self, call, waits=()):
        return self.op("vector", call, waits)

    def s(self, call, waits=()):
        return self.op("scalar", call, waits)

    def g(self, call, waits=()):
        return self.op("gpsimd", call, waits)

    def t(self, call, waits=(), inc=False):
        ch = self.chain["tensor"]
        incs = []
        if inc:
            ch.count += 1
            incs = [(ch, 1)]
        self._issue("tensor", call, waits, incs)
        return (ch, ch.count) if inc else None

    def t_tok(self):
        return (self.chain["tensor"], self.chain["tensor"].count)

    def dma(self, qname, out, in_, waits=(), **kw):
        def call(eng):
            return eng.dma_start(out=out, in_=in_, **kw)
        return self.dma_call(qname, call, waits)

    def dma_call(self, qname, call, waits=()):
        pool = self.dma_pools[qname]
        i = self.dma_rr[qname] % len(pool)
        self.dma_rr[qname] += 1
        sem = pool[i]
        w = list(waits)
        if sem.count > 0:
            w.append((sem, sem.count))
        sem.count += 16
        val = sem.count
        self._issue(qname, call, w, [(sem, 16)])
        return (sem, val)

    def dma_wait(self, eng_name, tok):
        self._issue(eng_name, None, [tok], [])

    def wait(self, eng_name, toks):
        self._issue(eng_name, None, list(toks), [])

    def barrier(self, engines=("scalar", "vector", "tensor", "gpsimd", "sync")):
        toks = []
        for e in ("scalar", "vector", "gpsimd", "tensor"):
            c = self.chain[e]
            if c.count:
                toks.append((c, c.count))
        for pool in self.dma_pools.values():
            for c in pool:
                if c.count:
                    toks.append((c, c.count))
        for e in engines:
            self._issue(e, None, toks, [])

    def finalize(self):
        assert not self._finalized
        self._finalized = True
        nc = self.nc
        streams = self.streams
        with nc.Block() as block:
            if streams["sync"]:
                @block.sync
                def _(eng):
                    for f in streams["sync"]:
                        f(eng)
            if streams["scalar"]:
                @block.scalar
                def _(eng):
                    for f in streams["scalar"]:
                        f(eng)
            if streams["vector"]:
                @block.vector
                def _(eng):
                    for f in streams["vector"]:
                        f(eng)
            if streams["tensor"]:
                @block.tensor
                def _(eng):
                    for f in streams["tensor"]:
                        f(eng)
            if streams["gpsimd"]:
                @block.gpsimd
                def _(eng):
                    for f in streams["gpsimd"]:
                        f(eng)
        self.stack.close()
        return nc


def _prod(xs):
    r = 1
    for x in xs:
        r *= x
    return r


class SbArena:

    def __init__(self, b: "KB", n_bytes_per_partition, name="sbarena"):
        assert n_bytes_per_partition % 2 == 0
        self.t = b.sb(name, [128, n_bytes_per_partition // 2], BF16)
        self.off = 0
        self.limit = n_bytes_per_partition

    def tile(self, shape, dtype):
        assert shape[0] <= 128
        elt = 2 if dtype.name == "bfloat16" else 4
        n = _prod(shape[1:]) * elt
        off = (self.off + 63) & ~63
        self.off = off + n
        assert self.off <= self.limit, (
            f"SbArena overflow: need {self.off}, have {self.limit}")
        v = self.t[0:shape[0], off // 2:(off + n) // 2]
        if elt == 4:
            v = v.bitcast(dtype)
        if len(shape) == 2:
            return v
        kw = {}
        names = []
        for i, s in enumerate(shape[1:]):
            names.append(f"d{i}")
            kw[f"d{i}"] = s
        return v.rearrange(f"p ({' '.join(names)}) -> p {' '.join(names)}", **kw)

    def mark(self):
        return self.off

    def reset(self, m=0):
        self.off = m


D = 1024
T = 4096
ND = D // 128
SL_CM = 512
NSL_CM = T // SL_CM


def ceil_div(a, b):
    return (a + b - 1) // b


def make_consts():
    out = {}
    out["ident128_bf16"] = np.eye(128, dtype=ml_dtypes.bfloat16)
    out["ident128_f32"] = np.eye(128, dtype=np.float32)
    out["ident64_bf16"] = np.eye(64, dtype=ml_dtypes.bfloat16)
    out["ones_col_bf16"] = np.ones((128, 1), dtype=ml_dtypes.bfloat16)
    out["ones_row_bf16"] = np.ones((1, 128), dtype=ml_dtypes.bfloat16)
    out["ones_row_f32"] = np.ones((1, 128), dtype=np.float32)
    slf = np.zeros((128, 1), dtype=np.float32)
    slf[127, 0] = 1
    out["sel_last_f32"] = slf
    s8 = np.zeros((8, 8, 128), dtype=ml_dtypes.bfloat16)
    for e in range(8):
        s8[e, e, :] = 1
    out["sel8_bf16"] = s8.reshape(8, 8 * 128)
    s16 = np.zeros((16, 16, 128), dtype=ml_dtypes.bfloat16)
    for e in range(16):
        s16[e, e, :] = 1
    out["sel16_bf16"] = s16.reshape(16, 16 * 128)
    s32 = np.zeros((32, 32, 128), dtype=ml_dtypes.bfloat16)
    for e in range(32):
        s32[e, e, :] = 1
    out["sel32_bf16"] = s32.reshape(32, 32 * 128)
    for n in (64, 128):
        r = np.arange(n)
        out[f"trilS{n}"] = (r[:, None] > r[None, :]).astype(np.float32)
        out[f"trilI{n}"] = (r[:, None] >= r[None, :]).astype(np.float32)
        out[f"triuS{n}"] = (r[:, None] < r[None, :]).astype(np.float32)
        out[f"triuI{n}"] = (r[:, None] <= r[None, :]).astype(np.float32)
    p = np.arange(128)
    s = np.arange(64)
    out["maskLx2"] = ((s[None, :] < (p % 64)[:, None])).astype(np.float32)
    out["maskUSx2"] = (((p % 64)[:, None] < s[None, :])).astype(np.float32)
    out["maskUIx2"] = (((p % 64)[:, None] <= s[None, :])).astype(np.float32)
    sh = np.zeros((16, 8, 128), dtype=ml_dtypes.bfloat16)
    for pd in range(8):
        sh[2 * pd, pd, 0:64] = 1
        sh[2 * pd + 1, pd, 64:128] = 1
    out["selhalf16"] = sh
    hm = np.zeros((128, 8, 16), dtype=ml_dtypes.bfloat16)
    for dt in range(8):
        hm[0:64, dt, 2 * dt] = 1
        hm[64:128, dt, 2 * dt + 1] = 1
    out["headmaskT"] = hm
    out["dup64"] = ((p[:, None] % 64) == (p[None, :] % 64)).astype(ml_dtypes.bfloat16)
    sl_ = np.zeros((128, 1), dtype=ml_dtypes.bfloat16)
    sl_[127, 0] = 1
    out["sel_last"] = sl_
    out["masklo"] = (p < 64).astype(np.float32)[:, None]
    out["maskhi"] = (p >= 64).astype(np.float32)[:, None]
    g8 = (p[:, None] // 8 == np.arange(16)[None, :]).astype(np.float32)
    out["grp8_f32"] = g8
    out["grp8T_f32"] = np.ascontiguousarray(g8.T)
    return out


class Consts:
    def __init__(self, b: KB, aps):
        self.t = {}
        self.toks = {}
        for name, ap in aps.items():
            tile = b.sb(f"const_{name}", ap.shape, ap.dtype)
            self.toks[name] = b.dma("sync", tile[:], ap[:])
            self.t[name] = tile

    def __getitem__(self, k):
        return self.t[k]

    def tok(self, k):
        return self.toks[k]


class Arena:

    def __init__(self, b: KB, name="psArena"):
        self.t = b.ps(name, [128, 4096], F32)
        self.bf = None
        self.last = {}

    def bank(self, i, rows=128, cols=512, row0=0):
        return self.t[row0:row0 + rows, 512 * i: 512 * i + cols]

    def span(self, i0, nbanks, rows=128):
        return self.t[0:rows, 512 * i0: 512 * (i0 + nbanks)]

    def bank_bf16(self, i, rows=128, cols=1024):
        v = self.t[:].bitcast(BF16)
        return v[0:rows, 1024 * i: 1024 * i + cols]

    def war(self, banks):
        out = []
        for bk in banks:
            tkn = self.last.get(bk)
            if tkn is not None:
                out.append(tkn)
        return out

    def use(self, banks, token):
        for bk in banks:
            self.last[bk] = token


def load_weight_km(b: KB, name, wap, qname="gpsimd", waits=()):
    k_dim, m_dim = wap.shape
    nk = ceil_div(k_dim, 128)
    tile = b.sb(f"w_{name}", [128, nk, m_dim], BF16)
    tok = b.dma(qname, tile[:], wap.rearrange("(k p) m -> p k m", p=128), waits=waits)
    return tile, tok


class LN4:

    def __init__(self, b: KB, c: Consts, tag, ar: Arena, bs1, bs2, bq, bc, nd=ND,
                 sl=SL_CM):
        self.b, self.c, self.tag, self.nd, self.ar = b, c, tag, nd, ar
        self.sl = sl
        self.bk = (bs1, bs2, bq, bc)
        self.ps_s1 = ar.bank(bs1, rows=1, cols=sl)
        self.ps_s2 = ar.bank(bs2, rows=1, cols=sl)
        self.ps_q = ar.bank(bq, rows=128, cols=sl)
        self.ps_c = ar.bank(bc, rows=128, cols=sl)
        self.rows = {k: b.sb(f"{tag}_row_{k}", [1, sl], F32)
                     for k in ("m", "v", "q", "cc")}
        self.rows_bf = {k: b.sb(f"{tag}_rowbf_{k}", [1, sl], BF16) for k in ("q", "cc")}
        self.sqbf = b.sb(f"{tag}_sqbf", [128, nd, sl], BF16)
        self.qrep = b.sb(f"{tag}_qrep", [128, sl], BF16)
        self.crep = b.sb(f"{tag}_crep", [128, sl], BF16)
        self.ones_bf = c["ones_col_bf16"]
        self.onesrow_bf = c["ones_row_bf16"]

    def slice(self, xbf, xbf_tok, u_out_sl, eps1=1e-5, eps2=None, wvec=None,
              dual=None):
        b, nd, ar = self.b, self.nd, self.ar
        bs1, bs2, bq, bc = self.bk
        war_s = ar.war([bs1, bs2])
        for i in range(nd):
            b.t(lambda t, i=i: t.matmul(self.ps_s1, self.ones_bf[:],
                                        xbf[:, i, :], start=(i == 0), stop=(i == nd - 1)),
                waits=([xbf_tok, self.c.tok("ones_col_bf16")] + war_s) if i == 0 else ())
        sq_toks = []
        for i in range(nd):
            sq_toks.append(b.s(lambda s, i=i: s.activation(
                self.sqbf[:, i, :], xbf[:, i, :], AF.Square),
                waits=[xbf_tok] if i == 0 else ()))
        stats_tok = None
        for i in range(nd):
            stats_tok = b.t(lambda t, i=i: t.matmul(
                self.ps_s2, self.ones_bf[:], self.sqbf[:, i, :],
                start=(i == 0), stop=(i == nd - 1)),
                waits=[sq_toks[i]] + (war_s if i == 0 else []), inc=(i == nd - 1))
        r = self.rows
        b.v(lambda v: v.tensor_scalar_mul(r["m"][:], self.ps_s1, 1.0 / D),
            waits=[stats_tok])
        b.v(lambda v: v.tensor_tensor(r["cc"][:], r["m"][:], r["m"][:], ALU.mult))
        b.v(lambda v: v.scalar_tensor_tensor(
            r["v"][:], self.ps_s2, 1.0 / D, r["cc"][:], ALU.mult, ALU.subtract))
        if dual is not None:
            qln, cln = dual
            dtok = b.v(lambda v: v.tensor_scalar_add(qln, r["v"][:], eps1))
            b.s(lambda s: s.activation(qln, qln, AF.Ln), waits=[dtok])
            d2 = b.s(lambda s: s.activation(qln, qln, AF.Exp, scale=-0.5))
            b.v(lambda v: v.scalar_tensor_tensor(
                cln, r["m"][:], -1.0, qln, ALU.mult, ALU.mult), waits=[d2])
        if eps2 is None:
            vtok = b.v(lambda v: v.tensor_scalar_add(r["v"][:], r["v"][:], eps1))
        else:
            vtok = b.v(lambda v: v.tensor_scalar(r["v"][:], r["v"][:], 1.0 + eps2,
                                                 eps1 * eps2, ALU.mult, ALU.add))
        b.s(lambda s: s.activation(r["q"][:], r["v"][:], AF.Ln), waits=[vtok])
        stok2 = b.s(lambda s: s.activation(r["q"][:], r["q"][:], AF.Exp, scale=-0.5))
        b.v(lambda v: v.scalar_tensor_tensor(
            r["cc"][:], r["m"][:], -1.0, r["q"][:], ALU.mult, ALU.mult),
            waits=[stok2])
        b.v(lambda v: v.tensor_copy(self.rows_bf["q"][:], r["q"][:]))
        cast_tok = b.v(lambda v: v.tensor_copy(self.rows_bf["cc"][:], r["cc"][:]))
        ar.use([bs1, bs2], cast_tok)
        war_b = ar.war([bq, bc])
        b.t(lambda t: t.matmul(self.ps_q, self.onesrow_bf[:], self.rows_bf["q"][:],
                               start=True, stop=True),
            waits=[cast_tok, self.c.tok("ones_row_bf16")] + war_b)
        bc_tok = b.t(lambda t: t.matmul(self.ps_c, self.onesrow_bf[:],
                                        self.rows_bf["cc"][:], start=True, stop=True),
                     inc=True)
        b.v(lambda v: v.tensor_copy(self.qrep[:], self.ps_q), waits=[bc_tok])
        ev_tok = b.v(lambda v: v.tensor_copy(self.crep[:], self.ps_c))
        ar.use([bq, bc], ev_tok)
        ap_tok = None
        for i in range(nd):
            tgt = u_out_sl[:, i, :]
            b.v(lambda v, i=i: v.tensor_tensor(
                self.sqbf[:, i, :], xbf[:, i, :], self.qrep[:], ALU.mult))
            if wvec is None:
                ap_tok = b.v(lambda v, i=i, tgt=tgt: v.tensor_tensor(
                    tgt, self.sqbf[:, i, :], self.crep[:], ALU.add))
            else:
                b.v(lambda v, i=i: v.tensor_tensor(
                    self.sqbf[:, i, :], self.sqbf[:, i, :], self.crep[:], ALU.add))
                ap_tok = b.v(lambda v, i=i, tgt=tgt: v.tensor_scalar_mul(
                    tgt, self.sqbf[:, i, :], wvec[:, i:i + 1]))
        return ap_tok


def emit_ln_full(b, c, ln: "LN4", x_fm, u_bf, eps1=1e-5, eps2=None, wvec=None,
                 xbf_keep=None):
    x_r = x_fm.rearrange("(dt p) t -> p dt t", p=128)
    ring = None
    if xbf_keep is None:
        ring = b.sb("lnxbf_ring", [128, 2, ND, SL_CM], BF16)
    prev_apply = {}
    tok = None
    for s in range(NSL_CM):
        if xbf_keep is None:
            dst = ring[:, s % 2, :, :]
            war = [prev_apply[s % 2]] if (s % 2) in prev_apply else []
        else:
            dst = xbf_keep[:, :, s * SL_CM:(s + 1) * SL_CM]
            war = []
        xtok = b.dma("gpsimd", dst, x_r[:, :, s * SL_CM:(s + 1) * SL_CM], waits=war)
        tok = ln.slice(dst, xtok, u_bf[:, :, s * SL_CM:(s + 1) * SL_CM],
                       eps1=eps1, eps2=eps2, wvec=wvec)
        prev_apply[s % 2] = tok
    return tok


TS_L0A = 512
NSEG_L0A = T // TS_L0A
NCH_L0A = TS_L0A // 64
NPT_L0A = TS_L0A // 128
SL_L0A = 512


def emit_l0attn(b: KB, c: Consts, ar: Arena, sa: SbArena, x_fm, out_fm, W,
                dbg=None, tag="l0a"):
    x_r = x_fm.rearrange("(dt p) t -> p dt t", p=128)
    o_r = out_fm.rearrange("(dt p) t -> p dt t", p=128)

    def ldw(nm, ap, qn="gpsimd"):
        k_dim, m_dim = ap.shape
        nk = k_dim // 128
        tile = sa.tile([128, nk, m_dim], BF16)
        tok = b.dma(qn, tile, ap.rearrange("(k p) m -> p k m", p=128))
        return tile, tok

    S_bf = b.sb(f"{tag}_Sbf", [128, 512], BF16)
    psS = ar.bank(7)
    qln_st = b.sb(f"{tag}_qlnst", [1, 2 * SL_L0A], F32)
    qc_dram = b.dram(f"{tag}_qc_dram", [NSEG_L0A, 2 * SL_L0A], F32)
    ln = LN4(b, c, f"{tag}_ln", ar, 0, 1, 2, 3, sl=SL_L0A)
    xbf_ring = b.sb(f"{tag}_xbfr", [128, 2, ND, SL_L0A], BF16)
    ln1 = b.sb(f"{tag}_ln1", [128, ND], F32)
    ln1_tok = b.dma("sync", ln1[:], W["ln1"].rearrange("(dt p) -> p dt", p=128),
                    allow_slow_non_contiguous=True)
    ln2 = b.sb(f"{tag}_ln2", [128, ND], F32)
    ln2_tok = b.dma("sync", ln2[:], W["ln2"].rearrange("(dt p) -> p dt", p=128),
                    allow_slow_non_contiguous=True)
    o_dram = b.dram(f"{tag}_o_dram", [1024, T], BF16)

    mark0 = sa.mark()
    wq, wq_tok = ldw("wq", W["Wq"])
    wk, wk_tok = ldw("wk", W["Wk"])
    wv, wv_tok = ldw("wv", W["Wv"])
    wb_, wb_tok = ldw("wb", W["Wb"])
    Ybufs = [sa.tile([128, 512], BF16) for _ in range(5)]
    Xp = sa.tile([128, 2, 512], BF16)
    tmpU = sa.tile([128, 2, 512], BF16)
    browm = sa.tile([128, 512], BF16)
    Gm = sa.tile([128, 512], BF16)
    Vsb = sa.tile([128, 512], BF16)
    Ksb = sa.tile([128, 512], BF16)
    dR = sa.tile([128, 512], F32)
    Osb = sa.tile([128, 512], BF16)
    xin = sa.tile([128, ND, TS_L0A], BF16)
    qfm = sa.tile([128, ND, TS_L0A], BF16)
    kfm = sa.tile([128, ND, TS_L0A], BF16)
    vfm = sa.tile([128, ND, TS_L0A], BF16)
    ofmst = sa.tile([128, ND, TS_L0A], BF16)
    beta32 = sa.tile([16, TS_L0A], F32)
    beta_bf = sa.tile([16, TS_L0A], BF16)
    btm = sa.tile([128, NPT_L0A, 16], BF16)
    Bstk = sa.tile([128, 8, NCH_L0A], F32)
    sqb = sa.tile([128, 2, TS_L0A], BF16)
    rho32 = sa.tile([16, TS_L0A], F32)
    rho_bf = sa.tile([16, TS_L0A], BF16)
    selb = sa.tile([128, 16], F32)

    idb = c["ident128_bf16"]
    id64 = c["ident64_bf16"]
    selh = c["selhalf16"]
    hmT = c["headmaskT"]
    dup = c["dup64"]
    mlo = c["masklo"]
    mhi = c["maskhi"]
    ones_bf = c["ones_col_bf16"]

    prev = {}
    prev["sbf"] = b.v(lambda v: v.memset(S_bf[:], 0.0))

    def do_segment(seg):
        t0 = seg * TS_L0A
        if seg == 0:
            b.wait("vector", [ln1_tok])
        if "dbgqkv" in prev:
            b.wait("scalar", prev["dbgqkv"])
            b.wait("vector", prev["dbgqkv"])
        if ("oseg", seg - 1) in prev:
            b.wait("vector", [prev[("oseg", seg - 1)]])
        par = seg % 2
        war = [prev[("xbf", par)]] if ("xbf", par) in prev else []
        xbf = xbf_ring[:, par, :, :]
        xtok = b.dma("gpsimd", xbf, x_r[:, :, t0:t0 + TS_L0A], waits=war)
        if ("lnuser", seg) in prev:
            b.wait("vector", [prev[("lnuser", seg)]])
        if "qcdma" in prev:
            b.wait("vector", [prev["qcdma"]])
        ln_tok = ln.slice(xbf, xtok, xin, eps1=1e-5, eps2=1e-6, wvec=ln1[:],
                          dual=(qln_st[:, 0:SL_L0A], qln_st[:, SL_L0A:2 * SL_L0A]))
        prev["qcdma"] = b.dma("sync", qc_dram[seg].unsqueeze(0), qln_st[:],
                              waits=[ln_tok])
        prev[("xbf", par)] = ln_tok

        ping = [4, 5, 6]
        it = 0
        ptoks = {}
        for name, wgt, wtok, dst in (("q", wq, wq_tok, qfm), ("k", wk, wk_tok, kfm),
                                     ("v", wv, wv_tok, vfm)):
            for dt in range(ND):
                bank = ping[it % 3]
                ps = ar.bank(bank)
                war2 = ar.war([bank])
                mm = None
                for kk in range(ND):
                    mm = b.t(lambda t, kk=kk, dt=dt, ps=ps, wgt=wgt: t.matmul(
                        ps, wgt[:, kk, dt * 128:(dt + 1) * 128], xin[:, kk, :],
                        start=(kk == 0), stop=(kk == ND - 1)),
                        waits=([ln_tok, wtok] + war2) if kk == 0 else (),
                        inc=(kk == ND - 1))
                ev = b.s(lambda s, ps=ps, dst=dst, dt=dt: s.activation(
                    dst[:, dt, :], ps, AF.Silu), waits=[mm])
                ar.use([bank], ev)
                ptoks[(name, dt)] = ev
                it += 1
        bank = ping[it % 3]
        it += 1
        psb16 = ar.bank(bank, rows=16)
        war2 = ar.war([bank])
        mm = None
        for kk in range(ND):
            mm = b.t(lambda t, kk=kk, psb16=psb16: t.matmul(
                psb16, wb_[:, kk, :], xin[:, kk, :],
                start=(kk == 0), stop=(kk == ND - 1)),
                waits=([ln_tok, wb_tok] + war2) if kk == 0 else (),
                inc=(kk == ND - 1))
        ev = b.s(lambda s, psb16=psb16: s.activation(beta32[0:16, :], psb16,
                                                     AF.Sigmoid), waits=[mm])
        ar.use([bank], ev)
        bcast_tok = b.v(lambda v: v.tensor_copy(beta_bf[0:16, :], beta32[0:16, :]),
                        waits=[ev])

        for name, dst in (("q", qfm), ("k", kfm)):
            bank_r = ping[it % 3]
            it += 1
            ps_rho = ar.bank(bank_r, rows=16, cols=TS_L0A)
            war2 = ar.war([bank_r])
            sq_toks = {}
            mm1 = None
            for dt in range(ND):
                stq = b.s(lambda s, dt=dt, dst=dst: s.activation(
                    sqb[:, dt % 2, :], dst[:, dt, :], AF.Square),
                    waits=([ptoks[(name, dt)]] +
                           ([sq_toks[dt - 2]] if dt - 2 in sq_toks else [])))
                mm1 = b.t(lambda t, dt=dt, ps_rho=ps_rho: t.matmul(
                    ps_rho, hmT[:, dt, :], sqb[:, dt % 2, :],
                    start=(dt == 0), stop=(dt == ND - 1)),
                    waits=([stq, c.tok("headmaskT")] + war2) if dt == 0 else [stq],
                    inc=True)
                sq_toks[dt] = mm1
            vt = b.v(lambda v, ps_rho=ps_rho: v.tensor_scalar_add(
                rho32[0:16, :], ps_rho, 1e-6), waits=[mm1])
            b.s(lambda s: s.activation(rho32[0:16, :], rho32[0:16, :], AF.Ln),
                waits=[vt])
            st2 = b.s(lambda s: s.activation(rho32[0:16, :], rho32[0:16, :],
                                             AF.Exp, scale=-0.5))
            cast = b.v(lambda v: v.tensor_copy(rho_bf[0:16, :], rho32[0:16, :]),
                       waits=[st2])
            ar.use([bank_r], cast)
            lastap = None
            for dt in range(ND):
                bank = ping[it % 3]
                it += 1
                ps = ar.bank(bank)
                war3 = ar.war([bank])
                mm = b.t(lambda t, dt=dt, ps=ps: t.matmul(
                    ps, selh[:, dt, :], rho_bf[0:16, :], start=True, stop=True),
                    waits=[cast, c.tok("selhalf16")] + war3, inc=True)
                lastap = b.v(lambda v, dt=dt, ps=ps, dst=dst: v.tensor_tensor(
                    dst[:, dt, :], dst[:, dt, :], ps, ALU.mult), waits=[mm])
                ar.use([bank], lastap)
            ptoks["n" + name] = lastap

        btoks = []
        for pt in range(NPT_L0A):
            bank = ping[it % 3]
            it += 1
            psb = ar.bank_bf16(bank, rows=128, cols=16)
            war2 = ar.war([bank])
            trr = b.t(lambda t, pt=pt, psb=psb: t.transpose(
                psb, beta_bf[0:16, pt * 128:(pt + 1) * 128], idb[0:16, 0:16]),
                waits=[bcast_tok, c.tok("ident128_bf16")] + war2, inc=True)
            evt = b.v(lambda v, pt=pt, psb=psb: v.tensor_copy(btm[:, pt, :], psb),
                      waits=[trr])
            ar.use([bank], evt)
            btoks.append(evt)
        sel2 = None
        for pd in range(8):
            for parr in range(2):
                bank = ping[it % 3]
                it += 1
                ps = ar.bank(bank, cols=2 * NPT_L0A)
                war2 = ar.war([bank])
                rhs = btm[parr * 64:(parr + 1) * 64, :, 2 * pd:2 * pd + 2]
                rhs = rhs.transpose([0, 2, 1])
                mm = b.t(lambda t, pd=pd, parr=parr, ps=ps, rhs=rhs: t.matmul(
                    ps, dup[parr * 64:(parr + 1) * 64, :], rhs,
                    start=True, stop=True),
                    waits=[btoks[-1], c.tok("dup64")] + war2, inc=True)
                tgt = Bstk[:, pd, :].rearrange("p (cc parr) -> p cc parr", parr=2)
                tgt = tgt[:, :, parr]
                b.v(lambda v, ps=ps: v.tensor_scalar_mul(
                    selb[:, 0:NPT_L0A], ps[:, 0:NPT_L0A], mlo[:]),
                    waits=[mm, c.tok("masklo")])
                sel2 = b.v(lambda v, ps=ps, tgt=tgt: v.scalar_tensor_tensor(
                    tgt, ps[:, NPT_L0A:2 * NPT_L0A], mhi[:], selb[:, 0:NPT_L0A],
                    ALU.mult, ALU.add), waits=[c.tok("maskhi")])
                ar.use([bank], sel2)
        bstk_tok = sel2

        def do_chunk(cc):
            tc = cc * 64
            psA, psG, psSv, psO = ar.bank(0), ar.bank(1), ar.bank(2), ar.bank(3)
            psX, psY, psW = ar.bank(4), ar.bank(5), ar.bank(6)
            psW_bf = ar.bank_bf16(6, cols=512)
            psA_bf = ar.bank_bf16(0, cols=512)
            psG_bf = ar.bank_bf16(1, cols=512)
            warA = ar.war([0])
            warG = ar.war([1])
            warSv = ar.war([2])
            warW = ar.war([6])
            a_mm = g_mm = sv_mm = br_mm = None
            for pd in range(8):
                for h in range(2):
                    lk = kfm[64 * h:64 * h + 64, pd, tc:tc + 64]
                    a_mm = b.t(lambda t, lk=lk, pd=pd, h=h: t.matmul(
                        psA[64 * h:64 * h + 64, pd * 64:(pd + 1) * 64], lk, lk,
                        start=True, stop=True),
                        waits=([ptoks["nk"]] + warA) if (pd == 0 and h == 0) else (),
                        inc=(pd == 7 and h == 1))
            for pd in range(8):
                for h in range(2):
                    lk = kfm[64 * h:64 * h + 64, pd, tc:tc + 64]
                    lq = qfm[64 * h:64 * h + 64, pd, tc:tc + 64]
                    g_mm = b.t(lambda t, lk=lk, lq=lq, pd=pd, h=h: t.matmul(
                        psG[64 * h:64 * h + 64, pd * 64:(pd + 1) * 64], lk, lq,
                        start=True, stop=True),
                        waits=([ptoks["nq"]] + warG) if (pd == 0 and h == 0) else (),
                        inc=(pd == 7 and h == 1))
                    sv_mm = b.t(lambda t, lk=lk, pd=pd, h=h: t.matmul(
                        psSv[64 * h:64 * h + 64, pd * 64:(pd + 1) * 64], lk,
                        S_bf[64 * h:64 * h + 64, pd * 64:(pd + 1) * 64],
                        start=True, stop=True),
                        waits=([prev["sbf"]] + warSv) if (pd == 0 and h == 0) else (),
                        inc=(pd == 7 and h == 1))
            for pd in range(8):
                br_mm = b.t(lambda t, pd=pd, tc=tc: t.matmul(
                    psW[:, pd * 64:(pd + 1) * 64], selh[:, pd, :],
                    beta_bf[0:16, tc:tc + 64], start=True, stop=True),
                    waits=([bstk_tok] + warW) if pd == 0 else (), inc=(pd == 7))
            bview = Bstk[:, :, cc].unsqueeze(2).to_broadcast([128, 8, 64])
            mUS = c["maskUSx2"][:].unsqueeze(1).to_broadcast([128, 8, 64])
            mL = c["maskLx2"][:].unsqueeze(1).to_broadcast([128, 8, 64])
            mUI = c["maskUIx2"][:].unsqueeze(1).to_broadcast([128, 8, 64])

            def v3(ap):
                return ap.rearrange("p (pd s) -> p pd s", pd=8)

            brt = b.v(lambda v: v.tensor_tensor(
                v3(browm[0:128, :]), psW.rearrange("p (pd s) -> p pd s", pd=8),
                mUS, ALU.mult), waits=[br_mm, c.tok("maskUSx2")])
            ar.use([6], brt)
            b.v(lambda v: v.tensor_tensor(
                v3(Xp[:, 0, :]), psA.rearrange("p (pd s) -> p pd s", pd=8),
                bview, ALU.mult), waits=[a_mm])
            m2 = b.v(lambda v: v.tensor_tensor(
                v3(Xp[:, 0, :]), v3(Xp[:, 0, :]), mL, ALU.mult),
                waits=[c.tok("maskLx2")])
            y0 = b.v(lambda v: v.tensor_tensor(
                v3(Ybufs[0][0:128, :]), psA.rearrange("p (pd s) -> p pd s", pd=8),
                v3(browm[0:128, :]), ALU.mult))
            ar.use([0], y0)
            gmt = b.v(lambda v: v.tensor_tensor(
                v3(Gm[0:128, :]), psG.rearrange("p (pd s) -> p pd s", pd=8),
                mUI, ALU.mult), waits=[g_mm, c.tok("maskUIx2")])
            ar.use([1], gmt)
            vt_tr = None
            for pd in range(8):
                for h in range(2):
                    vt_tr = b.t(lambda t, pd=pd, h=h, tc=tc: t.transpose(
                        psW_bf[64 * h:64 * h + 64, pd * 64:(pd + 1) * 64],
                        vfm[64 * h:64 * h + 64, pd, tc:tc + 64],
                        idb[64 * h:64 * h + 64, 64 * h:64 * h + 64],
                        tile_position=(64 * h, 64 * h)),
                        waits=([brt, ptoks[("v", pd)], c.tok("ident64_bf16")])
                        if (pd == 0 and h == 0) else (),
                        inc=(pd == 7 and h == 1))
            vev = b.v(lambda v: v.tensor_copy(Vsb[0:128, :], psW_bf[:, 0:512]),
                      waits=[vt_tr])
            ar.use([6], vev)
            dt1 = b.v(lambda v: v.tensor_tensor(dR[0:128, :], Vsb[0:128, :], psSv,
                                                ALU.subtract), waits=[sv_mm, vev])
            ar.use([2], dt1)
            rt = b.v(lambda v: v.tensor_tensor(
                v3(tmpU[:, 0, :]), v3(dR[0:128, :]), bview, ALU.mult),
                waits=[prev["dbgU"]] if "dbgU" in prev else ())
            if dbg is not None and "dbg_M" in dbg and seg == 0 and cc == 0:
                b.dma("sync", dbg["dbg_Bstk"][:], Bstk, waits=[bstk_tok])
                b.dma("sync", dbg["dbg_btm"][:], btm, waits=[bstk_tok])
                bvtok = b.v(lambda v: v.tensor_copy(
                    dR[0:128, :].rearrange("p (pd s) -> p pd s", pd=8), bview),
                    waits=[bstk_tok])
                dtk0 = b.dma("sync", dbg["dbg_bview"][:], dR[0:128, :],
                             waits=[bvtok])
                b.wait("vector", [dtk0])
                dtk = [b.dma("sync", dbg["dbg_M"][:], Xp[:, 0, :], waits=[m2]),
                       b.dma("sync", dbg["dbg_Y0"][:], Ybufs[0][0:128, :],
                             waits=[y0]),
                       b.dma("sync", dbg["dbg_R"][:], tmpU[:, 0, :], waits=[rt]),
                       b.dma("sync", dbg["dbg_Gm"][:], Gm[0:128, :], waits=[gmt])]
                b.wait("scalar", dtk)
                b.wait("vector", dtk)
            kt_tr = None
            for pd in range(8):
                for h in range(2):
                    kt_tr = b.t(lambda t, pd=pd, h=h, tc=tc: t.transpose(
                        psA_bf[64 * h:64 * h + 64, pd * 64:(pd + 1) * 64],
                        kfm[64 * h:64 * h + 64, pd, tc:tc + 64],
                        idb[64 * h:64 * h + 64, 64 * h:64 * h + 64],
                        tile_position=(64 * h, 64 * h)),
                        waits=[y0] if (pd == 0 and h == 0) else (),
                        inc=(pd == 7 and h == 1))
            kev = b.v(lambda v: v.tensor_copy(Ksb[0:128, :], psA_bf[:, 0:512]),
                      waits=[kt_tr])
            ar.use([0], kev)
            xcur_tok = m2
            ycur_tok = y0
            for lev in range(1, 5):
                xsrc = Xp[:, (lev + 1) % 2, :]
                xdst = Xp[:, lev % 2, :]
                ysrc = Ybufs[lev - 1]
                ydst = Ybufs[lev]
                warX = ar.war([4])
                warY = ar.war([5])
                mmx = mmy = None
                for pd in range(8):
                    for h in range(2):
                        sub = slice(64 * h, 64 * h + 64)
                        cols = slice(pd * 64, (pd + 1) * 64)
                        mmx = b.t(lambda t, sub=sub, cols=cols, ysrc=ysrc, xsrc=xsrc:
                                  t.matmul(psX[sub, cols], ysrc[sub, cols],
                                           xsrc[sub, cols], start=True, stop=True),
                                  waits=([xcur_tok, ycur_tok] + warX)
                                  if (pd == 0 and h == 0) else (),
                                  inc=(pd == 7 and h == 1))
                for pd in range(8):
                    for h in range(2):
                        sub = slice(64 * h, 64 * h + 64)
                        cols = slice(pd * 64, (pd + 1) * 64)
                        mmy = b.t(lambda t, sub=sub, cols=cols, ysrc=ysrc, xsrc=xsrc:
                                  t.matmul(psY[sub, cols], xsrc[sub, cols],
                                           ysrc[sub, cols], start=True, stop=True),
                                  waits=warY if (pd == 0 and h == 0) else (),
                                  inc=(pd == 7 and h == 1))
                xcur_tok = b.s(lambda s, xdst=xdst: s.copy(xdst, psX), waits=[mmx])
                ar.use([4], xcur_tok)
                ycur_tok = b.s(lambda s, ydst=ydst: s.copy(ydst[0:128, :], psY),
                               waits=[mmy])
                ar.use([5], ycur_tok)
            psAp = ar.bank(2)
            cur = 0
            ap_tok = rt
            warAp0 = ar.war([2])
            mma = None
            for pd in range(8):
                for h in range(2):
                    sub = slice(64 * h, 64 * h + 64)
                    cols = slice(pd * 64, (pd + 1) * 64)
                    mma = b.t(lambda t, sub=sub, cols=cols: t.matmul(
                        psAp[sub, cols], Ybufs[4][sub, cols],
                        tmpU[sub, 0, cols], start=True, stop=True),
                        waits=([ap_tok, ycur_tok, xcur_tok] + warAp0)
                        if (pd == 0 and h == 0) else (),
                        inc=(pd == 7 and h == 1))
            w1_tok = b.s(lambda s: s.copy(Xp[:, 1, :], psAp), waits=[mma])
            ar.use([2], w1_tok)
            warAp1 = ar.war([2])
            mma = None
            for pd in range(8):
                for h in range(2):
                    sub = slice(64 * h, 64 * h + 64)
                    cols = slice(pd * 64, (pd + 1) * 64)
                    mma = b.t(lambda t, sub=sub, cols=cols: t.matmul(
                        psAp[sub, cols], Ybufs[4][sub, cols],
                        Xp[sub, 1, cols], start=True, stop=True),
                        waits=([w1_tok] + warAp1) if (pd == 0 and h == 0) else (),
                        inc=(pd == 7 and h == 1))
            ap_tok = b.v(lambda v: v.tensor_tensor(
                tmpU[:, 1, :], tmpU[:, 0, :], psAp, ALU.add), waits=[mma])
            ar.use([2], ap_tok)
            cur = 1
            for ybuf, sign in ((Ybufs[4], 1), (Ybufs[3], 1), (Ybufs[2], 1),
                               (Ybufs[1], 1), (Ybufs[0], -1)):
                warAp = ar.war([2])
                mma = None
                for pd in range(8):
                    for h in range(2):
                        sub = slice(64 * h, 64 * h + 64)
                        cols = slice(pd * 64, (pd + 1) * 64)
                        mma = b.t(lambda t, sub=sub, cols=cols, ybuf=ybuf, cur=cur:
                                  t.matmul(psAp[sub, cols], ybuf[sub, cols],
                                           tmpU[sub, cur, cols], start=True,
                                           stop=True),
                                  waits=([ap_tok] + warAp)
                                  if (pd == 0 and h == 0) else (),
                                  inc=(pd == 7 and h == 1))
                op = ALU.add if sign > 0 else ALU.subtract
                ap_tok = b.v(lambda v, cur=cur, op=op: v.tensor_tensor(
                    tmpU[:, 1 - cur, :], tmpU[:, cur, :], psAp, op), waits=[mma])
                ar.use([2], ap_tok)
                cur = 1 - cur
            warO = ar.war([3])
            o_mm = None
            for pd in range(8):
                for h in range(2):
                    sub = slice(64 * h, 64 * h + 64)
                    cols = slice(pd * 64, (pd + 1) * 64)
                    lq = qfm[sub, pd, tc:tc + 64]
                    st = (pd == 0)
                    o_mm = b.t(lambda t, sub=sub, cols=cols, lq=lq, st=st: t.matmul(
                        psO[sub, cols], lq, S_bf[sub, cols], start=st, stop=False,
                        skip_group_check=True),
                        waits=([ap_tok] + warO) if (pd == 0 and h == 0) else ())
            for pd in range(8):
                for h in range(2):
                    sub = slice(64 * h, 64 * h + 64)
                    cols = slice(pd * 64, (pd + 1) * 64)
                    o_mm = b.t(lambda t, sub=sub, cols=cols, cur=cur: t.matmul(
                        psO[sub, cols], Gm[sub, cols], tmpU[sub, cur, cols],
                        start=False, stop=True, skip_group_check=True),
                        waits=[gmt] if (pd == 0 and h == 0) else (),
                        inc=(pd == 7 and h == 1))
            oev = b.v(lambda v: v.tensor_copy(Osb[0:128, :], psO), waits=[o_mm])
            ar.use([3], oev)
            s_mm = None
            war7 = ar.war([7])
            for pd in range(8):
                for h in range(2):
                    sub = slice(64 * h, 64 * h + 64)
                    cols = slice(pd * 64, (pd + 1) * 64)
                    st = (seg == 0 and cc == 0 and pd == 0)
                    s_mm = b.t(lambda t, sub=sub, cols=cols, cur=cur, st=st: t.matmul(
                        psS[sub, cols], Ksb[sub, cols], tmpU[sub, cur, cols],
                        start=st, stop=False, skip_group_check=True),
                        waits=([kev] + war7) if (pd == 0 and h == 0) else (),
                        inc=(pd == 7 and h == 1))
            sev = b.v(lambda v: v.tensor_copy(S_bf[:], psS),
                      waits=[s_mm] + ([prev["dbgS"]] if "dbgS" in prev else []))
            ar.use([7], sev)
            prev["sbf"] = sev
            if dbg is not None and "dbg_U" in dbg:
                ci = seg * NCH_L0A + cc
                prev["dbgU"] = b.dma("sync", dbg["dbg_U"][ci], tmpU[:, cur, :],
                                     waits=[ap_tok])
                prev["dbgS"] = b.dma("sync", dbg["dbg_S"][ci], S_bf[:],
                                     waits=[sev])
            ot_tr = None
            war1b = ar.war([1])
            for pd in range(8):
                for h in range(2):
                    sub = slice(64 * h, 64 * h + 64)
                    ot_tr = b.t(lambda t, pd=pd, h=h, sub=sub: t.transpose(
                        psG_bf[sub, pd * 64:(pd + 1) * 64],
                        Osb[sub, pd * 64:(pd + 1) * 64],
                        idb[64 * h:64 * h + 64, 64 * h:64 * h + 64],
                        tile_position=(64 * h, 64 * h)),
                        waits=([oev] + war1b) if (pd == 0 and h == 0) else (),
                        inc=(pd == 7 and h == 1))
            ofm_t = b.v(lambda v, tc=tc: v.tensor_copy(
                ofmst[:, :, tc:tc + 64],
                psG_bf[:, 0:512].rearrange("p (pd s) -> p pd s", pd=8)),
                waits=[ot_tr])
            ar.use([1], ofm_t)
            return ofm_t

        ofm_t = None
        for cc in range(NCH_L0A):
            ofm_t = do_chunk(cc)
        otok = b.dma("sync", o_dram.rearrange("(dt p) t -> p dt t", p=128)[
            :, :, t0:t0 + TS_L0A], ofmst, waits=[ofm_t])
        prev[("lnuser", seg + 1)] = b.t_tok()
        prev[("oseg", seg)] = otok
        if dbg is not None:
            dts = []
            for nm, buf in (("q", qfm), ("k", kfm), ("v", vfm)):
                if f"dbg_{nm}" in dbg:
                    dts.append(b.dma("sync", dbg[f"dbg_{nm}"].rearrange(
                        "(dt p) t -> p dt t", p=128)[:, :, t0:t0 + TS_L0A], buf,
                        waits=[ptoks["nq"], ptoks["nk"], ofm_t]))
            if "dbg_beta" in dbg:
                dts.append(b.dma("sync", dbg["dbg_beta"][:, t0:t0 + TS_L0A],
                                 beta32[0:16, :], waits=[ofm_t]))
            if dts:
                prev["dbgqkv"] = dts

    for seg in range(NSEG_L0A):
        do_segment(seg)

    delta_done = prev[("oseg", NSEG_L0A - 1)]
    if dbg is not None and "dbg_o" in dbg:
        b.dma("sync", dbg["dbg_o"][:, :], o_dram[:, :], waits=[delta_done])

    b.barrier()
    sa.reset(mark0)
    wo, wo_tok = ldw("wo", W["Wo"])
    wg_ring = sa.tile([128, 2, 8, 512], BF16)
    wu_ring = sa.tile([128, 2, 8, 512], BF16)
    wd_ring = sa.tile([128, 2, 4, 1024], BF16)
    hf = sa.tile([128, 32, SL_L0A], BF16)
    osl = sa.tile([128, ND, SL_L0A], BF16)
    ybf = osl
    x32b = sa.tile([128, ND, SL_L0A], F32)
    h2buf = sa.tile([128, ND, SL_L0A], F32)
    outb = sa.tile([128, ND, 256], F32)
    gbf2 = sa.tile([128, 2, SL_L0A], BF16)
    rowq2 = sa.tile([1, SL_L0A], F32)
    rowq2_bf = sa.tile([1, SL_L0A], BF16)
    rowc2_bf = sa.tile([1, SL_L0A], BF16)
    sq2 = sa.tile([128, 2, SL_L0A], BF16)
    onesrow = c["ones_row_bf16"]

    ring_war = {}
    wdr_war = {}
    w_toks = {}
    wd_toks = {}

    def prefetch_f(s, ft):
        if (s, ft) in w_toks or s >= 8:
            return
        slot = (s * 8 + ft) % 2
        war = [ring_war[slot]] if slot in ring_war else []
        tg = b.dma("gpsimd", wg_ring[:, slot, :, :],
                   W["Wg"][:, ft * 512:(ft + 1) * 512].rearrange(
                       "(k p) m -> p k m", p=128), waits=war)
        tu = b.dma("gpsimd", wu_ring[:, slot, :, :],
                   W["Wu"][:, ft * 512:(ft + 1) * 512].rearrange(
                       "(k p) m -> p k m", p=128))
        w_toks[(s, ft)] = (tg, tu, slot)

    def prefetch_wd(key, ft):
        if key in wd_toks:
            return
        slot = len(wd_toks) % 2
        war = [wdr_war[slot]] if slot in wdr_war else []
        td = b.dma("gpsimd", wd_ring[:, slot, :, :],
                   W["Wd"][ft * 512:(ft + 1) * 512, :].rearrange(
                       "(k p) m -> p k m", p=128), waits=war)
        wd_toks[key] = (td, slot)

    prefetch_f(0, 0)
    prefetch_f(0, 1)
    gbf_war = {}
    dn_prev = None
    otok_f = None

    def do_slice_c(s):
        nonlocal dn_prev, otok_f
        sl = slice(s * SL_L0A, (s + 1) * SL_L0A)
        war_x = [prev["cres"]] if "cres" in prev else []
        x32tok = b.dma("sync", x32b, x_r[:, :, sl], waits=war_x)
        war_o = [dn_prev] if dn_prev is not None else []
        obtok = b.dma("gpsimd", osl,
                      o_dram.rearrange("(dt p) t -> p dt t", p=128)[:, :, sl],
                      waits=war_o + [delta_done])
        if dn_prev is not None:
            b.wait("vector", [dn_prev])
        if "cact" in prev:
            b.wait("vector", [prev["cact"]])
        war_st = [prev["qcuse"]] if "qcuse" in prev else []
        ql_tok = b.dma("sync", qln_st[:], qc_dram[s].unsqueeze(0), waits=war_st)
        cst1 = b.v(lambda v: v.tensor_copy(rowq2_bf[0:1, :], qln_st[:, 0:SL_L0A]),
                   waits=[ql_tok])
        cst2 = b.v(lambda v: v.tensor_copy(rowc2_bf[0:1, :], qln_st[:, SL_L0A:2 * SL_L0A]))
        prev["qcuse"] = cst2
        war45 = ar.war([4, 5])
        b.t(lambda t: t.matmul(ar.bank(4), onesrow[:], rowq2_bf[0:1, :],
                               start=True, stop=True), waits=[cst1] + war45)
        bc_mm = b.t(lambda t: t.matmul(ar.bank(5), onesrow[:], rowc2_bf[0:1, :],
                                       start=True, stop=True),
                    waits=[cst2], inc=True)
        h2toks = []
        oWo_mm = None
        for dt in range(ND):
            bank = 6 + (dt % 2)
            ps = ar.bank(bank)
            war2 = ar.war([bank])
            mm = None
            for kk in range(ND):
                mm = b.t(lambda t, kk=kk, dt=dt, ps=ps: t.matmul(
                    ps, wo[:, kk, dt * 128:(dt + 1) * 128], osl[:, kk, :],
                    start=(kk == 0), stop=(kk == ND - 1)),
                    waits=([obtok, wo_tok, bc_mm] + war2) if kk == 0 else (),
                    inc=(kk == ND - 1))
            oWo_mm = mm
            b.v(lambda v, dt=dt: v.tensor_tensor(
                h2buf[:, dt, :], x32b[:, dt, :], ar.bank(4), ALU.mult),
                waits=[bc_mm, x32tok] if dt == 0 else ())
            b.v(lambda v, dt=dt: v.tensor_tensor(
                h2buf[:, dt, :], h2buf[:, dt, :], ar.bank(5), ALU.add))
            tk = b.v(lambda v, dt=dt, ps=ps: v.tensor_tensor(
                h2buf[:, dt, :], h2buf[:, dt, :], ps, ALU.add), waits=[mm])
            ar.use([bank], tk)
            h2toks.append(tk)
        ar.use([4, 5], h2toks[-1])
        war0 = ar.war([0])
        ps_r = ar.bank(0, rows=1)
        sqt = {}
        mm = None
        for dt in range(ND):
            stq = b.s(lambda ss, dt=dt: ss.activation(
                sq2[:, dt % 2, :], h2buf[:, dt, :], AF.Square),
                waits=[h2toks[dt]] + ([sqt[dt - 2]] if dt - 2 in sqt else []))
            mm = b.t(lambda t, dt=dt: t.matmul(
                ps_r, ones_bf[:], sq2[:, dt % 2, :],
                start=(dt == 0), stop=(dt == ND - 1)),
                waits=([stq] + war0) if dt == 0 else [stq],
                inc=True)
            sqt[dt] = mm
        vt = b.v(lambda v: v.tensor_scalar(rowq2[0:1, :], ps_r, 1.0 / D, 1e-6,
                                           ALU.mult, ALU.add), waits=[mm])
        b.s(lambda ss: ss.activation(rowq2[0:1, :], rowq2[0:1, :], AF.Ln),
            waits=[vt])
        e2 = b.s(lambda ss: ss.activation(rowq2[0:1, :], rowq2[0:1, :],
                                          AF.Exp, scale=-0.5))
        rc = b.v(lambda v: v.tensor_copy(rowq2_bf[0:1, :], rowq2[0:1, :]),
                 waits=[e2])
        ar.use([0], rc)
        war1 = ar.war([1])
        rep_mm = b.t(lambda t: t.matmul(ar.bank(1), onesrow[:], rowq2_bf[0:1, :],
                                        start=True, stop=True),
                     waits=[rc] + war1, inc=True)
        y_tok = None
        for dt in range(ND):
            b.v(lambda v, dt=dt: v.tensor_tensor(
                ybf[:, dt, :], h2buf[:, dt, :], ar.bank(1), ALU.mult),
                waits=[rep_mm] if dt == 0 else ())
            y_tok = b.v(lambda v, dt=dt: v.tensor_scalar_mul(
                ybf[:, dt, :], ybf[:, dt, :], ln2[:, dt:dt + 1]),
                waits=[ln2_tok] if dt == 0 else ())
        ar.use([1], y_tok)
        it2 = 0
        h_tok = None
        for fc in range(8):
            prefetch_f(s, fc)
            tg, tu, slot = w_toks[(s, fc)]
            for sub in range(4):
                msl = slice(sub * 128, (sub + 1) * 128)
                gp = (it2 % 2)
                up = 6 + (it2 % 2)
                gb_ps = ar.bank(gp)
                ub_ps = ar.bank(up)
                war_g = ar.war([gp])
                war_u = ar.war([up])
                gmm = None
                for kk in range(ND):
                    gmm = b.t(lambda t, kk=kk, gb_ps=gb_ps, slot=slot, msl=msl:
                              t.matmul(
                        gb_ps, wg_ring[:, slot, kk, msl], ybf[:, kk, :],
                        start=(kk == 0), stop=(kk == ND - 1)),
                        waits=([tg, tu, y_tok] + war_g)
                        if (kk == 0 and sub == 0) else (war_g if kk == 0 else ()),
                        inc=(kk == ND - 1))
                umm = None
                for kk in range(ND):
                    umm = b.t(lambda t, kk=kk, ub_ps=ub_ps, slot=slot, msl=msl:
                              t.matmul(
                        ub_ps, wu_ring[:, slot, kk, msl], ybf[:, kk, :],
                        start=(kk == 0), stop=(kk == ND - 1)),
                        waits=war_u if kk == 0 else (), inc=(kk == ND - 1))
                gpar = it2 % 2
                war_gbf = [gbf_war[gpar]] if gpar in gbf_war else []
                sg = b.s(lambda ss, gb_ps=gb_ps, gpar=gpar: ss.activation(
                    gbf2[:, gpar, :], gb_ps, AF.Silu), waits=[gmm] + war_gbf)
                fi = fc * 4 + sub
                h_tok = b.v(lambda v, ub_ps=ub_ps, gpar=gpar, fi=fi: v.tensor_tensor(
                    hf[:, fi, :], gbf2[:, gpar, :], ub_ps, ALU.mult),
                    waits=[sg, umm])
                gbf_war[gpar] = h_tok
                ring_war[slot] = umm
                ar.use([gp], h_tok)
                ar.use([up], h_tok)
                it2 += 1
            if fc + 2 < 8:
                prefetch_f(s, fc + 2)
            else:
                prefetch_f(s + 1, fc + 2 - 8)
        h_all_tok = h_tok
        res_toks = []
        otoks = []
        war_out = [prev["cout"]] if "cout" in prev else []
        for half in range(2):
            hs = slice(half * 256, (half + 1) * 256)
            y_ps = ar.span(2, 4)
            war_y = ar.war([2, 3, 4, 5])
            first_in_bank = [True] * 4
            dn_tok = None
            for fc in range(8):
                prefetch_wd((s, half, fc), fc)
                td, slot = wd_toks[(s, half, fc)]
                for kk in range(4):
                    fi = fc * 4 + kk
                    for d in range(ND):
                        bank_i = d // 2
                        st = first_in_bank[bank_i]
                        first_in_bank[bank_i] = False
                        w0 = ()
                        if fc == 0 and kk == 0 and d == 0:
                            w0 = [h_all_tok, td] + war_y
                        elif kk == 0 and d == 0:
                            w0 = [td]
                        dn_tok = b.t(lambda t, d=d, kk=kk, slot=slot, st=st,
                                     fi=fi, hs=hs: t.matmul(
                            y_ps[:, d * 256:(d + 1) * 256],
                            wd_ring[:, slot, kk, d * 128:(d + 1) * 128],
                            hf[:, fi, hs], start=st, stop=False,
                            skip_group_check=True),
                            waits=w0,
                            inc=(kk == 3 and d == ND - 1))
                wdr_war[slot] = dn_tok
            rtk = None
            war_ob = [otoks[-1]] if otoks else war_out
            for d in range(ND):
                b.v(lambda v, d=d, hs=hs: v.tensor_tensor(
                    outb[:, d, :], h2buf[:, d, hs],
                    y_ps[:, d * 256:(d + 1) * 256], ALU.add),
                    waits=([dn_tok] + war_ob) if d == 0 else ())
                rtk = b.v(lambda v, d=d, hs=hs: v.tensor_tensor(
                    outb[:, d, :], outb[:, d, :], x32b[:, d, hs], ALU.add))
            ar.use([2, 3, 4, 5], rtk)
            res_toks.append(rtk)
            otoks.append(b.dma("sync", o_r[:, :, sl][:, :, hs], outb,
                               waits=[rtk] + war_out))
            dn_prev = dn_tok
        prev["cact"] = (b.chain["scalar"], b.chain["scalar"].count)
        prev["cres"] = res_toks[-1]
        otok_f = otoks[-1]
        prev["cout"] = otok_f

    for s in range(8):
        do_slice_c(s)
    return otok_f


D_L0F = 1024
T_L0F = 4096
SL_L0F = 512
NSL_L0F = 8
ND_L0F = 8
NE_L0F = 16
CAP_L0F = 512
F_L0F = 2048
NFT_L0F = 16
NLINK_L0F = 11
OOB_L0F = 1.0e6


def emit_l0ffn(b: KB, c: Consts, ar: Arena, sa: SbArena, x_fm, out_fm, W,
               dbg=None, tag="l0f"):
    import os as _os2
    _SER2 = bool(_os2.environ.get("L0F_SERIAL2"))
    _SKIP_CD = bool(_os2.environ.get("L0F_SKIP_CD"))
    _SKIP_LINKS = bool(_os2.environ.get("L0F_SKIP_LINKS"))
    prev = {}
    x_r = x_fm.rearrange("(dt p) t -> p dt t", p=128)

    lnxw = b.dram(f"{tag}_lnxw", [T_L0F, D_L0F], BF16)
    ytok = b.dram(f"{tag}_ytok", [T_L0F, D_L0F], F32)
    wg_bf = b.dram(f"{tag}_wg", [NE_L0F, F_L0F, D_L0F], BF16)
    wu_bf = b.dram(f"{tag}_wu", [NE_L0F, F_L0F, D_L0F], BF16)
    wd_bf = b.dram(f"{tag}_wd", [NE_L0F, D_L0F, F_L0F], BF16)
    wsg_bf = b.dram(f"{tag}_wsg", [F_L0F, D_L0F], BF16)
    wsu_bf = b.dram(f"{tag}_wsu", [F_L0F, D_L0F], BF16)
    wsd_bf = b.dram(f"{tag}_wsd", [D_L0F, F_L0F], BF16)

    cast_g, cast_u, cast_d = {}, {}, {}
    cast_s = {}

    def ensure_cast(e):
        if e in cast_g:
            return
        cast_g[e] = b.dma("gpsimd", wg_bf[e], W["Wg"][e])
        cast_u[e] = b.dma("gpsimd", wu_bf[e], W["Wu"][e])
        cast_d[e] = b.dma("gpsimd", wd_bf[e], W["Wd"][e])

    def ensure_cast_s(i):
        if i in cast_s:
            return
        key = (("Ws_g", wsg_bf), ("Ws_u", wsu_bf), ("Ws_d", wsd_bf))[i]
        cast_s[i] = b.dma("gpsimd", key[1][:], W[key[0]][:])

    def vfence():
        vch = b.chain["vector"]
        return [(vch, vch.count)] if vch.count else []

    zb = sa.tile([128, D_L0F], F32)
    mz = b.v(lambda v: v.memset(zb[:], 0.0))
    ztoks = []
    sio = sa.tile([128, CAP_L0F], F32)
    sctok = sa.tile([128, 32, NE_L0F], BF16)
    slot_tok = sa.tile([128, 32, NE_L0F], F32)
    rk_tok = sa.tile([128, 32, NE_L0F], F32)
    m_R = sa.mark()

    sio_i = sa.tile([128, CAP_L0F], I32)
    g_io = b.g(lambda g: g.iota(sio_i[:], pattern=[[1, CAP_L0F]], base=0,
                                channel_multiplier=0))
    sio_tok = b.v(lambda v: v.tensor_copy(sio[:], sio_i[:]), waits=[g_io])

    gw_sb = sa.tile([NE_L0F, D_L0F], F32)
    t_gw = b.dma("sync", gw_sb[:], W["gate_w"][:])
    id16 = c["ident128_f32"][0:16, 0:16]
    gwT_sb = sa.tile([128, ND_L0F, NE_L0F], F32)
    war7 = ar.war([7])
    tk = None
    for kt in range(ND_L0F):
        tk = b.t(lambda t, kt=kt: t.matmul(
            ar.bank(7, cols=128)[:, kt * 16:(kt + 1) * 16],
            gw_sb[:, kt * 128:(kt + 1) * 128], id16,
            start=True, stop=True, skip_group_check=True),
            waits=([t_gw, c.tok("ident128_f32")] + war7) if kt == 0 else (),
            inc=(kt == ND_L0F - 1))
    gwT_tok = b.v(lambda v: v.tensor_copy(
        gwT_sb[:].rearrange("p k e -> p (k e)"), ar.bank(7, cols=128)),
        waits=[tk])
    ar.use([7], gwT_tok)
    gsum = sa.tile([NE_L0F, 1], F32)
    grow = sa.tile([1, NE_L0F], F32)
    grep = sa.tile([128, NE_L0F], F32)
    gs_tok = b.v(lambda v: v.reduce_sum(gsum[:], gw_sb[:], axis=AX.X),
                 waits=[t_gw])
    war7 = ar.war([7])
    tk = b.t(lambda t: t.matmul(ar.bank(7, rows=1, cols=16), gsum[:], id16,
                                start=True, stop=True, skip_group_check=True),
             waits=[gs_tok] + war7, inc=True)
    grow_tok = b.v(lambda v: v.tensor_copy(grow[:], ar.bank(7, rows=1, cols=16)),
                   waits=[tk])
    ar.use([7], grow_tok)
    war7 = ar.war([7])
    tk = b.t(lambda t: t.matmul(ar.bank(7, cols=16), c["ones_row_f32"][:],
                                grow[:], start=True, stop=True,
                                skip_group_check=True),
             waits=[grow_tok, c.tok("ones_row_f32")] + war7, inc=True)
    grep_tok = b.v(lambda v: v.tensor_copy(grep[:], ar.bank(7, cols=16)),
                   waits=[tk])
    ar.use([7], grep_tok)

    qc = sa.tile([1, 2 * T_L0F], F32)
    scores_em = sa.tile([NE_L0F, T_L0F], F32)
    xf32 = sa.tile([128, ND_L0F, SL_L0F], F32)
    xbf = sa.tile([128, ND_L0F, SL_L0F], BF16)
    lnx = sa.tile([128, ND_L0F, SL_L0F], BF16)
    lnxw_sg = sa.tile([128, 4, D_L0F], BF16)
    qct = sa.tile([128, 8], F32)
    scr = sa.tile([128, 4, NE_L0F], F32)
    sco = sa.tile([128, 4, NE_L0F], F32)
    mx = sa.tile([128, 4, 1], F32)
    sm = sa.tile([128, 4, 1], F32)
    ln = LN4(b, c, tag, ar, 2, 3, 4, 5)
    st_toks = []

    def do_seg(seg):
        bL = 0 if seg % 2 == 0 else 6
        xt = b.dma("scalar", xf32[:], x_r[:, :, seg * SL_L0F:(seg + 1) * SL_L0F],
                   waits=prev.get("xf32_use", []))
        ct = None
        for i in range(ND_L0F):
            ct = b.v(lambda v, i=i: v.tensor_copy(xbf[:, i, :], xf32[:, i, :]),
                     waits=[xt] if i == 0 else ())
        warL = ar.war([bL])
        tokL = None
        for j in range(4):
            for kt in range(ND_L0F):
                first = (kt == 0 and j == 0)
                tokL = b.t(lambda t, kt=kt, j=j: t.matmul(
                    ar.bank(bL, cols=64)[:, j * 16:(j + 1) * 16],
                    xf32[:, kt, j * 128:(j + 1) * 128], gwT_sb[:, kt, :],
                    start=(kt == 0), stop=(kt == ND_L0F - 1),
                    skip_group_check=True),
                    waits=([xt, gwT_tok] + warL) if first else (),
                    inc=(kt == ND_L0F - 1 and j == 3))
        if "lnx_pe" in prev:
            b.wait("vector", [prev["lnx_pe"]])
        ap_tok = ln.slice(xbf[:], ct, lnx[:], eps1=1e-5,
                          dual=(qc[0:1, seg * SL_L0F:(seg + 1) * SL_L0F],
                                qc[0:1, T_L0F + seg * SL_L0F: T_L0F + (seg + 1) * SL_L0F]))
        war7b = ar.war([7])
        tkq = None
        for j in range(4):
            tkq = b.t(lambda t, j=j, seg=seg: t.matmul(
                ar.bank(7, cols=8)[:, j:j + 1],
                qc[0:1, seg * SL_L0F + j * 128: seg * SL_L0F + (j + 1) * 128],
                c["ones_row_f32"][0:1, 0:1], start=True, stop=True,
                skip_group_check=True),
                waits=([ap_tok] + war7b) if j == 0 else ())
            tkq = b.t(lambda t, j=j, seg=seg: t.matmul(
                ar.bank(7, cols=8)[:, 4 + j:5 + j],
                qc[0:1, T_L0F + seg * SL_L0F + j * 128: T_L0F + seg * SL_L0F + (j + 1) * 128],
                c["ones_row_f32"][0:1, 0:1], start=True, stop=True,
                skip_group_check=True), inc=(j == 3))
        qct_tok = b.v(lambda v: v.tensor_copy(qct[:], ar.bank(7, cols=8)),
                      waits=[tkq])
        ar.use([7], qct_tok)
        psL3 = ar.bank(bL, cols=64).rearrange("p (j e) -> p j e", j=4)
        sread = b.v(lambda v: v.tensor_tensor(
            scr[:], psL3, qct[:, 0:4].unsqueeze(2).to_broadcast([128, 4, NE_L0F]),
            ALU.mult), waits=[tokL, qct_tok])
        ar.use([bL], sread)
        b.v(lambda v: v.tensor_tensor(
            sco[:], qct[:, 4:8].unsqueeze(2).to_broadcast([128, 4, NE_L0F]),
            grep[:].unsqueeze(1).to_broadcast([128, 4, NE_L0F]), ALU.mult),
            waits=([grep_tok] if seg == 0 else []) + prev.get("sco_pe", []))
        b.v(lambda v: v.tensor_tensor(scr[:], scr[:], sco[:], ALU.add))
        b.v(lambda v: v.reduce_max(mx[:], scr[:], axis=AX.X))
        sub_tok = b.v(lambda v: v.tensor_tensor(
            scr[:], scr[:], mx[:].to_broadcast([128, 4, NE_L0F]), ALU.subtract))
        exp_tok = b.s(lambda s: s.activation(scr[:], scr[:], AF.Exp),
                      waits=[sub_tok])
        b.v(lambda v: v.reduce_sum(sm[:], scr[:], axis=AX.X), waits=[exp_tok])
        b.v(lambda v: v.reciprocal(sm[:], sm[:]))
        sco_tok = b.v(lambda v: v.tensor_tensor(
            sco[:], scr[:], sm[:].to_broadcast([128, 4, NE_L0F]), ALU.mult))
        b.v(lambda v, seg=seg: v.tensor_copy(sctok[:, seg * 4:(seg + 1) * 4, :],
                                             sco[:]))
        war_sg = [prev["lnxw_sg_st"]] if "lnxw_sg_st" in prev else []
        lnx_pe = None
        for kt in range(ND_L0F):
            war1 = ar.war([1])
            tk2 = None
            for j in range(4):
                tk2 = b.t(lambda t, kt=kt, j=j: t.matmul(
                    ar.bank(1)[:, j * 128:(j + 1) * 128],
                    lnx[:, kt, j * 128:(j + 1) * 128], c["ident128_bf16"][:],
                    start=True, stop=True, skip_group_check=True),
                    waits=([ap_tok, c.tok("ident128_bf16")] + war1)
                    if j == 0 else (), inc=(j == 3))
            lnx_pe = tk2
            ev = b.v(lambda v, kt=kt: v.tensor_copy(
                lnxw_sg[:, :, kt * 128:(kt + 1) * 128],
                ar.bank(1).rearrange("p (j w) -> p j w", j=4)),
                waits=[tk2] + (war_sg if kt == 0 else []))
            ar.use([1], ev)
        prev["lnx_pe"] = lnx_pe
        st = b.dma("sync",
                   lnxw[seg * SL_L0F:(seg + 1) * SL_L0F, :].rearrange(
                       "(j p) w -> p j w", p=128),
                   lnxw_sg[:], waits=[ev])
        st_toks.append(st)
        prev["lnxw_sg_st"] = st
        war1 = ar.war([1])
        tk3 = None
        for j in range(4):
            tk3 = b.t(lambda t, j=j: t.matmul(
                ar.bank(1, rows=16)[:, j * 128:(j + 1) * 128],
                sco[:, j, :], c["ident128_f32"][:],
                start=True, stop=True, skip_group_check=True),
                waits=([sco_tok] + war1) if j == 0 else (), inc=(j == 3))
        prev["sco_pe"] = [tk3]
        ev2 = b.v(lambda v, seg=seg: v.tensor_copy(
            scores_em[:, seg * SL_L0F:(seg + 1) * SL_L0F], ar.bank(1, rows=16)),
            waits=[tk3])
        ar.use([1], ev2)
        prev["xf32_use"] = [tokL, ct]
        prev["scores_done"] = ev2

    for seg in range(NSL_L0F):
        do_seg(seg)
        if _SER2:
            b.barrier()
        for q in range(4):
            tt = seg * 4 + q
            ztoks.append(b.dma("sync", ytok[tt * 128:(tt + 1) * 128, :],
                               zb[:], waits=[mz]))
        if seg % 2 == 1:
            ensure_cast(seg // 2)

    scdup = sa.tile([128, SL_L0F], F32)
    t_dup = b.dma("gpsimd", scdup[:],
                  scores_em[:].rearrange("p (c w) -> p c w", c=8),
                  waits=[prev["scores_done"]])
    lo = sa.tile([128, 1], I32)
    pwf = sa.tile([128, 31], F32)
    pwi = sa.tile([128, 31], I32)
    ge_ch = sa.tile([128, SL_L0F], F32)
    pcnt = sa.tile([128, 1], F32)
    ind16 = sa.tile([NE_L0F, 1], F32)
    cand = sa.tile([128, 1], I32)
    stp = sa.tile([128, 1], F32)
    stpi = sa.tile([128, 1], I32)
    b.v(lambda v: v.memset(lo[:], 0))
    for k in range(31):
        b.v(lambda v, k=k: v.memset(pwf[:, k:k + 1], float(2 ** (30 - k))))
    pw_tok = b.v(lambda v: v.tensor_copy(pwi[:], pwf[:]))
    ps_cnt = ar.bank(7, rows=16, cols=1)
    ps_ind = ar.t[0:128, 512 * 7 + 1: 512 * 7 + 2]

    def do_iter(i):
        b.v(lambda v, i=i: v.tensor_tensor(cand[:], lo[:], pwi[:, i:i + 1],
                                           ALU.add),
            waits=[pw_tok] if i == 0 else ())
        gt = b.v(lambda v: v.tensor_scalar(ge_ch[:], scdup[:],
                                           cand[:].bitcast(F32), None,
                                           ALU.is_ge),
                 waits=[t_dup] if i == 0 else ())
        rt = b.v(lambda v: v.reduce_sum(pcnt[:], ge_ch[:], axis=AX.X))
        w7 = ar.war([7]) if i == 0 else \
            ([prev["bis_ind"]] if "bis_ind" in prev else [])
        tkc = b.t(lambda t: t.matmul(ps_cnt, c["grp8_f32"][:], pcnt[:],
                                     start=True, stop=True,
                                     skip_group_check=True),
                  waits=[rt, c.tok("grp8_f32")] + w7, inc=True)
        it = b.v(lambda v: v.tensor_scalar(ind16[:], ps_cnt, 512.0, None,
                                           ALU.is_ge), waits=[tkc])
        tki = b.t(lambda t: t.matmul(ps_ind, c["grp8T_f32"][:], ind16[:],
                                     start=True, stop=True,
                                     skip_group_check=True),
                  waits=[it, c.tok("grp8T_f32")] if i == 0 else [it],
                  inc=True)
        b.v(lambda v, i=i: v.tensor_scalar(stp[:], ps_ind,
                                           float(2 ** (30 - i)), None,
                                           ALU.mult), waits=[tki])
        b.v(lambda v: v.tensor_copy(stpi[:], stp[:]))
        lt = b.v(lambda v: v.tensor_tensor(lo[:], lo[:], stpi[:], ALU.add))
        prev["bis_ind"] = lt

    for i in range(31):
        do_iter(i)
    ar.use([7], prev["bis_ind"])

    loem = sa.tile([NE_L0F, 8], I32)
    t_em = b.dma("gpsimd", loem[:], lo[:], waits=[prev["bis_ind"]])
    thrf = loem[:, 0:1].bitcast(F32)
    A = sa.tile([NE_L0F, T_L0F], F32)
    Bq = sa.tile([NE_L0F, T_L0F], F32)
    Cc = sa.tile([NE_L0F, T_L0F], F32)
    nge = sa.tile([NE_L0F, 1], F32)
    neq = sa.tile([NE_L0F, 1], F32)
    nrem = sa.tile([NE_L0F, 1], F32)
    b.v(lambda v: v.tensor_scalar(A[:], scores_em[:], thrf, None, ALU.is_ge),
        waits=[t_em])
    b.v(lambda v: v.tensor_scalar(Bq[:], scores_em[:], thrf, None,
                                  ALU.is_equal))
    b.v(lambda v: v.tensor_tensor_scan(Cc[:], Bq[:], Bq[:], 0.0, ALU.add,
                                       ALU.bypass))
    b.v(lambda v: v.reduce_sum(nge[:], A[:], axis=AX.X))
    b.v(lambda v: v.reduce_sum(neq[:], Bq[:], axis=AX.X))
    b.v(lambda v: v.tensor_tensor(nrem[:], nge[:], neq[:], ALU.subtract))
    b.v(lambda v: v.tensor_scalar(nrem[:], nrem[:], -1.0, 512.0, ALU.mult,
                                  ALU.add))
    b.v(lambda v: v.tensor_scalar(Cc[:], Cc[:], nrem[:], None, ALU.is_le))
    b.v(lambda v: v.tensor_tensor(Cc[:], Cc[:], Bq[:], ALU.mult))
    b.v(lambda v: v.tensor_tensor(A[:], A[:], Bq[:], ALU.subtract))
    b.v(lambda v: v.tensor_tensor(A[:], A[:], Cc[:], ALU.add))
    b.v(lambda v: v.tensor_tensor_scan(Cc[:], A[:], A[:], 0.0, ALU.add,
                                       ALU.bypass))
    b.v(lambda v: v.scalar_tensor_tensor(A[:], A[:], -OOB_L0F, Cc[:], ALU.mult,
                                         ALU.add))
    slot_done = b.v(lambda v: v.tensor_scalar(A[:], A[:], OOB_L0F - 1.0, None,
                                              ALU.add))
    dbg_toks = []
    if dbg is not None and "dbg_slot" in dbg:
        dbg_toks.append(b.dma("sync", dbg["dbg_slot"][:], A[:],
                              waits=[slot_done]))
        dbg_toks.append(b.dma("sync", dbg["dbg_scores"][:], scores_em[:],
                              waits=[slot_done]))
    last_tok = None
    for tt in range(32):
        bk = 1 if tt % 2 == 0 else 6
        war = ar.war([bk])
        tk4 = b.t(lambda t, tt=tt, bk=bk: t.matmul(
            ar.bank(bk, cols=32)[:, 0:16], A[:, tt * 128:(tt + 1) * 128], id16,
            start=True, stop=True, skip_group_check=True),
            waits=[slot_done] + war, inc=False)
        tk4 = b.t(lambda t, tt=tt, bk=bk: t.matmul(
            ar.bank(bk, cols=32)[:, 16:32], Cc[:, tt * 128:(tt + 1) * 128],
            id16, start=True, stop=True, skip_group_check=True), inc=True)
        ev3 = b.v(lambda v, tt=tt, bk=bk: v.tensor_copy(
            slot_tok[:, tt, :], ar.bank(bk, cols=32)[:, 0:16]), waits=[tk4])
        last_tok = b.v(lambda v, tt=tt, bk=bk: v.tensor_copy(
            rk_tok[:, tt, :], ar.bank(bk, cols=32)[:, 16:32]))
        ar.use([bk], last_tok)
    prev["rt_done"] = last_tok

    fence_E = [prev["rt_done"]] + dbg_toks
    sa.reset(m_R)
    eo5 = [sa.tile([128, 4, D_L0F], BF16) for _ in range(NE_L0F - NLINK_L0F)]
    m_E5 = sa.mark()
    lnt = [sa.tile([128, D_L0F], BF16) for _ in range(2)]
    oh = [sa.tile([128, CAP_L0F], BF16) for _ in range(2)]
    geb = [sa.tile([128, CAP_L0F], BF16) for _ in range(2)]
    xT = sa.tile([128, ND_L0F, CAP_L0F], BF16)
    wgb = [sa.tile([128, ND_L0F, 512], BF16) for _ in range(2)]
    wub = [sa.tile([128, ND_L0F, 512], BF16) for _ in range(2)]
    wdT = sa.tile([128, NFT_L0F, D_L0F], BF16)
    hbf = sa.tile([128, NFT_L0F, CAP_L0F], BF16)
    htmp = [sa.tile([128, CAP_L0F], F32) for _ in range(2)]
    h2 = [sa.tile([128, CAP_L0F], F32) for _ in range(2)]
    wrep = [sa.tile([128, CAP_L0F], F32) for _ in range(2)]
    wrow = sa.tile([1, CAP_L0F], BF16)
    idrow = sa.tile([1, CAP_L0F], F32)
    idcol = sa.tile([128, 4], F32)
    idi = [[sa.tile([128, 1], I32) for _ in range(4)] for _ in range(2)]
    etok = [sa.tile([128, D_L0F], F32) for _ in range(2)]
    prev["link"] = ztoks[:]

    def do_expert(e):
        if e + 4 < NE_L0F:
            ensure_cast(e + 4)
        if e >= 12 and e - 12 < 3:
            ensure_cast_s(e - 12)
        r = e % 2
        fv = fence_E if e < 2 else []
        warG = ar.war(list(range(8)))
        gtok = None
        for tt in range(32):
            lw = prev.get(("lnt", tt % 2), [])
            tl = b.dma("sync" if tt % 2 == 0 else "scalar",
                       lnt[tt % 2][:], lnxw[tt * 128:(tt + 1) * 128, :],
                       waits=[st_toks[tt // 4]] + lw +
                       (fv if tt < 2 else []))
            ohw = prev.get(("oh", tt % 2), [])
            ot = b.v(lambda v, tt=tt, e=e: v.tensor_scalar(
                oh[tt % 2][:], sio[:], slot_tok[:, tt, e:e + 1], None,
                ALU.is_equal),
                waits=([sio_tok] if e == 0 and tt == 0 else []) + ohw +
                (fv if tt == 0 else []))
            for dt in range(8):
                gtok = b.t(lambda t, tt=tt, dt=dt: t.matmul(
                    ar.bank(dt), lnt[tt % 2][:, dt * 128:(dt + 1) * 128],
                    oh[tt % 2][:], start=(tt == 0), stop=(tt == 31)),
                    waits=([tl, ot] + (warG if tt == 0 else []))
                    if dt == 0 else (),
                    inc=(dt == 7))
            prev[("lnt", tt % 2)] = [gtok]
            prev[("oh", tt % 2)] = [gtok]
        xtoks = []
        for dt in range(8):
            xe = b.v(lambda v, dt=dt: v.tensor_copy(xT[:, dt, :], ar.bank(dt)),
                     waits=[gtok] if dt == 0 else ())
            xtoks.append(xe)
            ar.use([dt], xe)
        if _SER2:
            b.barrier()
        if dbg is not None and "dbg_xt" in dbg and e == 10:
            b.dma("sync", dbg["dbg_xt"][:], xT[:], waits=[xtoks[-1]])
        war01 = ar.war([6, 7])
        wtok = itok = None
        for tt in range(32):
            ohw = prev.get(("oh", tt % 2), [])
            ot = b.v(lambda v, tt=tt, e=e: v.tensor_scalar(
                oh[tt % 2][:], sio[:], slot_tok[:, tt, e:e + 1], None,
                ALU.is_equal), waits=ohw)
            gw = prev.get(("geb", tt % 2), [])
            mt_ = b.v(lambda v, tt=tt, e=e: v.tensor_scalar(
                geb[tt % 2][:], sio[:], rk_tok[:, tt, e:e + 1], None,
                ALU.is_ge), waits=gw)
            wtok = b.t(lambda t, tt=tt, e=e: t.matmul(
                ar.bank(6, rows=1, cols=CAP_L0F), sctok[:, tt, e:e + 1],
                oh[tt % 2][:], start=(tt == 0), stop=(tt == 31),
                skip_group_check=True),
                waits=([ot] + (war01 if tt == 0 else [])), inc=True)
            itok = b.t(lambda t, tt=tt: t.matmul(
                ar.bank(7, rows=1, cols=CAP_L0F), c["ones_col_bf16"][:],
                geb[tt % 2][:], start=(tt == 0), stop=(tt == 31),
                skip_group_check=True),
                waits=([mt_, c.tok("ones_col_bf16")] if tt == 0 else [mt_]),
                inc=True)
            prev[("oh", tt % 2)] = [wtok]
            prev[("geb", tt % 2)] = [itok]
        wr_ev = b.v(lambda v: v.tensor_copy(
            wrow[:], ar.bank(6, rows=1, cols=CAP_L0F)),
            waits=[wtok, itok] + prev.get(("wrow",), []))
        id_ev = b.v(lambda v: v.tensor_copy(
            idrow[:], ar.bank(7, rows=1, cols=CAP_L0F)))
        ar.use([6], id_ev)
        ar.use([7], id_ev)
        if dbg is not None and "dbg_idrow" in dbg and e == 10:
            b.dma("sync", dbg["dbg_idrow"][:], idrow[:], waits=[id_ev])
            b.dma("sync", dbg["dbg_wrow"][:], wrow[:], waits=[id_ev])
        war6 = ar.war([6])
        tk5 = b.t(lambda t: t.matmul(ar.bank(6), c["ones_row_bf16"][:],
                                     wrow[:], start=True, stop=True,
                                     skip_group_check=True),
                  waits=[wr_ev, c.tok("ones_row_bf16")] + war6, inc=True)
        wr_tok = b.v(lambda v, r=r: v.tensor_copy(wrep[r][:], ar.bank(6)),
                     waits=[tk5] + prev.get(("wrep", r), []))
        ar.use([6], wr_tok)
        prev[("wrow",)] = [tk5]
        id_toks = []
        if e < NLINK_L0F:
            ps_idc = ar.t[0:128, 512 * 7: 512 * 7 + 4]
            war6b = ar.war([7])
            tkq2 = None
            for q in range(4):
                tkq2 = b.t(lambda t, q=q: t.matmul(
                    ps_idc[:, q:q + 1],
                    idrow[0:1, q * 128:(q + 1) * 128],
                    c["ones_row_f32"][0:1, 0:1], start=True, stop=True,
                    skip_group_check=True),
                    waits=([id_ev] + war6b) if q == 0 else (), inc=(q == 3))
            prev[("wrow",)] = prev[("wrow",)] + [tkq2]
            for q in range(4):
                tq = b.v(lambda v, q=q, r=r: v.tensor_copy(
                    idi[r][q][:], ps_idc[:, q:q + 1]),
                    waits=([tkq2] if q == 0 else []) +
                    prev.get(("idi", r), []))
                id_toks.append(tq)
            ar.use([7], id_toks[-1])
        if _SER2:
            b.barrier()
        h_toks = [None] * NFT_L0F
        wg_waits = prev.get(("wgb", r), [])
        for fb in range(4):
            gtoks = [b.dma("sync", wgb[r][:, kt, :],
                           wg_bf[e, fb * 512:(fb + 1) * 512,
                                 kt * 128:(kt + 1) * 128], transpose=True,
                           waits=[cast_g[e]] + wg_waits + fv)
                     for kt in range(ND_L0F)]
            utoks = [b.dma("sync", wub[r][:, kt, :],
                           wu_bf[e, fb * 512:(fb + 1) * 512,
                                 kt * 128:(kt + 1) * 128], transpose=True,
                           waits=[cast_u[e]] + wg_waits)
                     for kt in range(ND_L0F)]
            wg_waits_new = []
            for sub in range(4):
                ft = fb * 4 + sub
                bg, bu = ft % 2, 2 + ft % 2
                warg = ar.war([bg])
                waru = ar.war([bu])
                tokg = toku = None
                for kt in range(ND_L0F):
                    tokg = b.t(lambda t, kt=kt, sub=sub, r=r, bg=bg: t.matmul(
                        ar.bank(bg), wgb[r][:, kt, sub * 128:(sub + 1) * 128],
                        xT[:, kt, :], start=(kt == 0), stop=(kt == ND_L0F - 1)),
                        waits=(gtoks + xtoks + warg) if kt == 0 else (),
                        inc=(kt == ND_L0F - 1))
                for kt in range(ND_L0F):
                    toku = b.t(lambda t, kt=kt, sub=sub, r=r, bu=bu: t.matmul(
                        ar.bank(bu), wub[r][:, kt, sub * 128:(sub + 1) * 128],
                        xT[:, kt, :], start=(kt == 0), stop=(kt == ND_L0F - 1)),
                        waits=(utoks + waru) if kt == 0 else (),
                        inc=(kt == ND_L0F - 1))
                hp = ft % 2
                ts = b.s(lambda s, hp=hp, bg=bg: s.activation(
                    htmp[hp][:], ar.bank(bg), AF.Gelu),
                    waits=[tokg] + prev.get(("htmp", hp), []))
                tv1 = b.v(lambda v, hp=hp, bu=bu: v.tensor_tensor(
                    h2[hp][:], htmp[hp][:], ar.bank(bu), ALU.mult),
                    waits=[ts, toku])
                prev[("htmp", hp)] = [tv1]
                tv2 = b.v(lambda v, ft=ft, hp=hp, r=r: v.tensor_tensor(
                    hbf[:, ft, :], h2[hp][:], wrep[r][:], ALU.mult),
                    waits=([wr_tok] + prev.get(("hbfdump",), []))
                    if ft == 0 else ())
                ar.use([bg], tv1)
                ar.use([bu], tv1)
                h_toks[ft] = tv2
                wg_waits_new = [tv2]
                if _os2.environ.get("L0F_SERIAL4"):
                    b.barrier()
            wg_waits = wg_waits_new
            if _os2.environ.get("L0F_SERIAL3"):
                b.barrier()
        prev[("wgb", r)] = [h_toks[NFT_L0F - 1]]
        prev[("wrep", r)] = [h_toks[NFT_L0F - 1]]
        if dbg is not None and "dbg_hbf" in dbg:
            hdt = b.dma("sync", dbg["dbg_hbf"][e], hbf[:],
                        waits=[h_toks[NFT_L0F - 1]])
            prev[("hbfdump",)] = [hdt]
        if _SER2:
            b.barrier()
        wd_waits = prev.get(("wdT",), [])
        wdtoks = [b.dma("sync", wdT[:, kt, :],
                        wd_bf[e, :, kt * 128:(kt + 1) * 128], transpose=True,
                        waits=[cast_d[e]] + wd_waits + fv)
                  for kt in range(NFT_L0F)]
        if dbg is not None and "dbg_wdt" in dbg and e == 10:
            b.dma("scalar", dbg["dbg_wdt"][:], wdT[:], waits=wdtoks)
        dn_last = None
        for mt in range(4):
            ep = mt % 2
            ev_toks = []
            for nt in range(2):
                be = 4 + (mt * 2 + nt) % 2
                ware = ar.war([be])
                tokd = None
                for kt in range(NFT_L0F):
                    tokd = b.t(lambda t, kt=kt, mt=mt, nt=nt, be=be:
                               t.matmul(
                                   ar.bank(be),
                                   hbf[:, kt, mt * 128:(mt + 1) * 128],
                                   wdT[:, kt, nt * 512:(nt + 1) * 512],
                                   start=(kt == 0), stop=(kt == NFT_L0F - 1)),
                               waits=[h_toks[kt], wdtoks[kt]] +
                               (ware if kt == 0 else []),
                               inc=(kt == NFT_L0F - 1))
                if e < NLINK_L0F:
                    ev = b.v(lambda v, ep=ep, nt=nt, be=be: v.tensor_copy(
                        etok[ep][:, nt * 512:(nt + 1) * 512], ar.bank(be)),
                        waits=[tokd] + (prev.get(("etok", ep), [])
                                        if nt == 0 else []))
                else:
                    j = e - NLINK_L0F
                    ev = b.v(lambda v, j=j, mt=mt, nt=nt, be=be: v.tensor_copy(
                        eo5[j][:, mt, nt * 512:(nt + 1) * 512], ar.bank(be)),
                        waits=[tokd])
                ar.use([be], ev)
                ev_toks.append(ev)
                if _os2.environ.get("L0F_SERIAL4"):
                    b.barrier()
            if e < NLINK_L0F and not _SKIP_LINKS:
                lk = b.dma_call("gpsimd", lambda g, mt=mt, ep=ep, r=r:
                                g.indirect_dma_start(
                                    out=ytok[:],
                                    out_offset=IndirectOffsetOnAxis(
                                        idi[r][mt][:], 0),
                                    in_=etok[ep][:], in_offset=None,
                                    bounds_check=T_L0F - 1, oob_is_err=False,
                                    compute_op=ALU.add),
                                waits=ev_toks + [id_toks[mt]] + prev["link"])
                prev["link"] = [lk]
                prev[("etok", ep)] = [lk]
                if dbg is not None and "dbg_eolk" in dbg:
                    dl = b.dma("scalar",
                               dbg["dbg_eolk"][e, mt * 128:(mt + 1) * 128, :],
                               etok[ep][:], waits=[lk])
                    prev[("etok", ep)] = [lk, dl]
                dn_last = lk
            else:
                dn_last = ev_toks[-1]
        if e < NLINK_L0F:
            prev[("idi", r)] = [dn_last]
        prev[("wdT",)] = [dn_last]

    import os as _os
    for e in range(NE_L0F):
        do_expert(e)
        if _os.environ.get("L0F_SERIAL"):
            b.barrier()
    link_last = prev["link"]

    sa.reset(m_E5)
    fCD = vfence() + link_last
    oh2 = [sa.tile([128, CAP_L0F], BF16) for _ in range(2)]
    ytile = [sa.tile([128, D_L0F], F32) for _ in range(2)]
    ohT = [sa.tile([128, 4, 128], BF16) for _ in range(2)]
    ycmb = [sa.tile([128, D_L0F], F32) for _ in range(2)]
    cmb_toks = []

    def do_cd(tt):
        r = tt % 2
        yw = prev.get(("ytile", r), [])
        tl = b.dma("sync", ytile[r][:], ytok[tt * 128:(tt + 1) * 128, :],
                   waits=fCD + yw)
        last_mm = None
        for j in range(NE_L0F - NLINK_L0F):
            e = NLINK_L0F + j
            ohw = prev.get(("oh2", j % 2), [])
            ot = b.v(lambda v, tt=tt, e=e, j=j: v.tensor_scalar(
                oh2[j % 2][:], sio[:], slot_tok[:, tt, e:e + 1], None,
                ALU.is_equal),
                waits=ohw + (fCD if tt == 0 and j < 2 else []))
            war0 = ar.war([0])
            tkT = None
            for sT in range(4):
                tkT = b.t(lambda t, sT=sT, j=j: t.matmul(
                    ar.bank(0)[:, sT * 128:(sT + 1) * 128],
                    oh2[j % 2][:, sT * 128:(sT + 1) * 128],
                    c["ident128_bf16"][:], start=True, stop=True,
                    skip_group_check=True),
                    waits=([ot] + war0) if sT == 0 else (), inc=(sT == 3))
            ohe = b.v(lambda v, j=j: v.tensor_copy(
                ohT[j % 2][:].rearrange("p q w -> p (q w)"), ar.bank(0)),
                waits=[tkT] + prev.get(("ohT", j % 2), []))
            ar.use([0], ohe)
            prev[("oh2", j % 2)] = [ohe]
            for sT in range(4):
                for nt in range(2):
                    bk = 1 + nt
                    first = (j == 0 and sT == 0)
                    last_mm = b.t(lambda t, j=j, sT=sT, nt=nt, bk=bk:
                                  t.matmul(
                                      ar.bank(bk), ohT[j % 2][:, sT, :],
                                      eo5[j][:, sT, nt * 512:(nt + 1) * 512],
                                      start=(j == 0 and sT == 0),
                                      stop=(j == NE_L0F - NLINK_L0F - 1 and sT == 3)),
                                  waits=([ohe] + (ar.war([bk]) if first
                                                  else [])),
                                  inc=(sT == 3 and nt == 1))
            prev[("ohT", j % 2)] = [last_mm]
        yv = None
        for nt in range(2):
            if _SKIP_CD:
                yv = b.v(lambda v, r=r, nt=nt: v.tensor_copy(
                    ycmb[r][:, nt * 512:(nt + 1) * 512],
                    ytile[r][:, nt * 512:(nt + 1) * 512]),
                    waits=([last_mm, tl] + prev.get(("ycmb", r), []))
                    if nt == 0 else ())
                continue
            yv = b.v(lambda v, r=r, nt=nt: v.tensor_tensor(
                ycmb[r][:, nt * 512:(nt + 1) * 512], ar.bank(1 + nt),
                ytile[r][:, nt * 512:(nt + 1) * 512], ALU.add),
                waits=([last_mm, tl] + prev.get(("ycmb", r), []))
                if nt == 0 else ())
        ar.use([1], yv)
        ar.use([2], yv)
        stc = b.dma("sync", ytok[tt * 128:(tt + 1) * 128, :], ycmb[r][:],
                    waits=[yv])
        cmb_toks.append(stc)
        prev[("ytile", r)] = [stc]
        prev[("ycmb", r)] = [stc]

    for tt in range(32):
        do_cd(tt)
        if _SER2:
            b.barrier()

    dbg_toks2 = []
    if dbg is not None and "dbg_eo5" in dbg:
        dbg_toks2.append(b.dma("sync", dbg["dbg_eo5"][:], eo5[0][:],
                               waits=[cmb_toks[-1]]))

    sa.reset(m_R)
    fS = vfence() + [cmb_toks[-1]] + dbg_toks2
    wsgT = sa.tile([128, ND_L0F, F_L0F], BF16)
    wsuT = sa.tile([128, ND_L0F, F_L0F], BF16)
    wsdT = sa.tile([128, NFT_L0F, D_L0F], BF16)
    wsg_toks = [b.dma("sync", wsgT[:, kt, :], wsg_bf[:, kt * 128:(kt + 1) * 128],
                      transpose=True, waits=[cast_s[0]] + fS)
                for kt in range(ND_L0F)]
    wsu_toks = [b.dma("sync", wsuT[:, kt, :], wsu_bf[:, kt * 128:(kt + 1) * 128],
                      transpose=True, waits=[cast_s[1]] + fS)
                for kt in range(ND_L0F)]
    wsd_toks = [b.dma("sync", wsdT[:, kt, :], wsd_bf[:, kt * 128:(kt + 1) * 128],
                      transpose=True, waits=[cast_s[2]] + fS)
                for kt in range(NFT_L0F)]
    lnxT = [sa.tile([128, ND_L0F, SL_L0F], BF16) for _ in range(2)]
    hsbf = sa.tile([128, NFT_L0F, SL_L0F], BF16)
    hst = [sa.tile([128, SL_L0F], F32) for _ in range(2)]
    hs2 = [sa.tile([128, SL_L0F], F32) for _ in range(2)]
    xres = [sa.tile([128, SL_L0F], F32) for _ in range(2)]
    ytk = [sa.tile([128, 128], F32) for _ in range(8)]
    obuf = [sa.tile([128, SL_L0F], F32) for _ in range(2)]

    def do_cc(cc):
        r = cc % 2
        lw = prev.get(("lnxT", r), [])
        ltoks = [b.dma("sync", lnxT[r][:, kt, :],
                       lnxw[cc * SL_L0F:(cc + 1) * SL_L0F, kt * 128:(kt + 1) * 128],
                       transpose=True, waits=st_toks + lw + fS)
                 for kt in range(ND_L0F)]
        hs_toks = [None] * NFT_L0F
        for ft in range(NFT_L0F):
            bg, bu = ft % 2, 2 + ft % 2
            warg = ar.war([bg])
            waru = ar.war([bu])
            tokg = toku = None
            for kt in range(ND_L0F):
                tokg = b.t(lambda t, kt=kt, ft=ft, r=r, bg=bg: t.matmul(
                    ar.bank(bg), wsgT[:, kt, ft * 128:(ft + 1) * 128],
                    lnxT[r][:, kt, :], start=(kt == 0), stop=(kt == ND_L0F - 1)),
                    waits=(wsg_toks + ltoks + warg) if kt == 0 else (),
                    inc=(kt == ND_L0F - 1))
            for kt in range(ND_L0F):
                toku = b.t(lambda t, kt=kt, ft=ft, r=r, bu=bu: t.matmul(
                    ar.bank(bu), wsuT[:, kt, ft * 128:(ft + 1) * 128],
                    lnxT[r][:, kt, :], start=(kt == 0), stop=(kt == ND_L0F - 1)),
                    waits=(wsu_toks + waru) if kt == 0 else (),
                    inc=(kt == ND_L0F - 1))
            hp = ft % 2
            ts = b.s(lambda s, hp=hp, bg=bg: s.activation(
                hst[hp][:], ar.bank(bg), AF.Gelu),
                waits=[tokg] + prev.get(("hst", hp), []))
            tv1 = b.v(lambda v, hp=hp, bu=bu: v.tensor_tensor(
                hs2[hp][:], hst[hp][:], ar.bank(bu), ALU.mult),
                waits=[ts, toku])
            prev[("hst", hp)] = [tv1]
            tv2 = b.v(lambda v, ft=ft, hp=hp: v.tensor_copy(
                hsbf[:, ft, :], hs2[hp][:]))
            ar.use([bg], tv1)
            ar.use([bu], tv1)
            hs_toks[ft] = tv2
        for dt in range(8):
            bd = 4 + dt % 2
            ward = ar.war([bd])
            tokd = None
            for kt in range(NFT_L0F):
                tokd = b.t(lambda t, kt=kt, dt=dt, bd=bd: t.matmul(
                    ar.bank(bd), wsdT[:, kt, dt * 128:(dt + 1) * 128],
                    hsbf[:, kt, :], start=(kt == 0), stop=(kt == NFT_L0F - 1)),
                    waits=[hs_toks[kt], wsd_toks[kt]] +
                    (ward if kt == 0 else []),
                    inc=(kt == NFT_L0F - 1))
            xw = prev.get(("xres", dt % 2), [])
            xt2 = b.dma("scalar", xres[dt % 2][:],
                        x_r[:, dt, cc * SL_L0F:(cc + 1) * SL_L0F],
                        waits=xw + (fS if cc == 0 and dt < 2 else []))
            yl = []
            for q in range(4):
                yw = prev.get(("ytk", dt % 2, q), [])
                yl.append(b.dma("sync", ytk[(dt % 2) * 4 + q][:],
                                ytok[cc * SL_L0F + q * 128: cc * SL_L0F + (q + 1) * 128,
                                     dt * 128:(dt + 1) * 128],
                                waits=[cmb_toks[cc * 4 + q]] + yw))
            war6 = ar.war([6])
            tk6 = None
            for q in range(4):
                tk6 = b.t(lambda t, q=q, dt=dt: t.matmul(
                    ar.bank(6)[:, q * 128:(q + 1) * 128],
                    ytk[(dt % 2) * 4 + q][:], c["ident128_f32"][:],
                    start=True, stop=True, skip_group_check=True),
                    waits=([yl[q]] + (war6 if q == 0 else [])), inc=(q == 3))
            ov0 = b.v(lambda v, dt=dt: v.tensor_copy(obuf[dt % 2][:],
                                                     ar.bank(6)),
                      waits=[tk6] + prev.get(("obuf", dt % 2), []))
            ar.use([6], ov0)
            ov = b.v(lambda v, dt=dt, bd=bd: v.tensor_tensor(
                obuf[dt % 2][:], obuf[dt % 2][:], ar.bank(bd), ALU.add),
                waits=[tokd])
            ar.use([bd], ov)
            ov2 = b.v(lambda v, dt=dt: v.tensor_tensor(
                obuf[dt % 2][:], obuf[dt % 2][:], xres[dt % 2][:], ALU.add),
                waits=[xt2])
            so = b.dma("sync", out_fm[dt * 128:(dt + 1) * 128,
                                      cc * SL_L0F:(cc + 1) * SL_L0F],
                       obuf[dt % 2][:], waits=[ov2])
            prev[("obuf", dt % 2)] = [so]
            prev[("xres", dt % 2)] = [so]
            for q in range(4):
                prev[("ytk", dt % 2, q)] = [tk6]
        prev[("lnxT", r)] = [hs_toks[NFT_L0F - 1]]

    for cc in range(8):
        do_cc(cc)
    if dbg is not None and "dbg_ytok" in dbg:
        b.dma("sync", dbg["dbg_ytok"][:], ytok[:], waits=[cmb_toks[-1]])

    if dbg is not None and "dbg_lnxw" in dbg:
        b.dma("sync", dbg["dbg_lnxw"][:], lnxw[:], waits=st_toks)


TS_L1A = 256
NSEG_L1A = T // TS_L1A
CH_L1A = 128
NCH_L1A = TS_L1A // CH_L1A
NZ_L1A = 16
NXB_L1A = 18
DIN_L1A = 2048


def emit_l1attn(b: KB, c: Consts, ar: Arena, sa: SbArena, x_fm, out_fm, W,
                dbg=None, tag="l1a"):
    x_r = x_fm.rearrange("(dt p) t -> p dt t", p=128)
    o_r = out_fm.rearrange("(dt p) t -> p dt t", p=128)

    wip_bf = b.dram(f"{tag}_wip_bf", [4384, 1024], BF16)
    wop_bf = b.dram(f"{tag}_wop_bf", [1024, 2048], BF16)
    cast1 = b.dma("gpsimd", wip_bf[:], W["in_proj_w"][:])
    cast2 = b.dma("gpsimd", wop_bf[:], W["out_proj_w"][:])
    acs_dram = b.dram(f"{tag}_acs_dram", [32, TS_L1A], F32)

    nrm = b.sb(f"{tag}_nrm", [128, ND], F32)
    nrm_tok = b.dma("sync", nrm[:], W["norm_w"].rearrange("(dt p) -> p dt", p=128),
                    allow_slow_non_contiguous=True)
    convw = b.sb(f"{tag}_convw", [128, NXB_L1A, 4], F32)
    convw_tok = b.dma("sync", convw[:],
                      W["conv_w"].rearrange("(kt p) one four -> p kt (one four)",
                                            p=128))
    convb = b.sb(f"{tag}_convb", [128, NXB_L1A], F32)
    convb_tok = b.dma("sync", convb[:],
                      W["conv_b"].rearrange("(kt p) -> p kt", p=128),
                      allow_slow_non_contiguous=True)
    dtb = b.sb(f"{tag}_dtb", [32, 1], F32)
    dtb_tok = b.dma("sync", dtb[:], W["dt_bias"].unsqueeze(1))
    alog = b.sb(f"{tag}_alog", [32, 1], F32)
    alog_tok = b.dma("sync", alog[:], W["A_log"].unsqueeze(1))
    drow = b.sb(f"{tag}_drow", [1, 32], F32)
    drow_tok = b.dma("sync", drow[:], W["D"].unsqueeze(0))
    gnrow = b.sb(f"{tag}_gnrow", [1, 2048], BF16)
    gn_tok = b.dma("gpsimd", gnrow[:], W["gnorm_w"].unsqueeze(0))

    onesrow = c["ones_row_bf16"]
    onesrow_f = c["ones_row_f32"]
    sel_last_f = c["sel_last_f32"]
    idb = c["ident128_bf16"]
    idf = c["ident128_f32"]
    triuI = c["triuI128"]

    A_neg = b.sb(f"{tag}_Aneg", [32, 1], F32)
    t1 = b.s(lambda s: s.activation(A_neg[:], alog[:], AF.Exp), waits=[alog_tok])
    aneg_tok = b.v(lambda v: v.tensor_scalar_mul(A_neg[:], A_neg[:], -1.0),
                   waits=[t1])

    D_rep = b.sb(f"{tag}_Drep", [128, 32], F32)
    gnw_rep = b.sb(f"{tag}_gnwrep", [128, 2048], BF16)
    war0 = ar.war([0])
    mm = b.t(lambda t: t.matmul(ar.bank(0, cols=32), onesrow_f[:], drow[:],
                                start=True, stop=True),
             waits=[drow_tok, c.tok("ones_row_f32")] + war0, inc=True)
    drep_tok = b.v(lambda v: v.tensor_copy(D_rep[:], ar.bank(0, cols=32)),
                   waits=[mm])
    ar.use([0], drep_tok)
    gn_last = None
    for j in range(4):
        bank = 1 + (j % 2)
        warj = ar.war([bank])
        mm = b.t(lambda t, j=j, bank=bank: t.matmul(
            ar.bank(bank), onesrow[:], gnrow[:, j * 512:(j + 1) * 512],
            start=True, stop=True),
            waits=[gn_tok, c.tok("ones_row_bf16")] + warj, inc=True)
        gn_last = b.v(lambda v, j=j, bank=bank: v.tensor_copy(
            gnw_rep[:, j * 512:(j + 1) * 512], ar.bank(bank)), waits=[mm])
        ar.use([bank], gn_last)

    wdt_fm = b.sb(f"{tag}_wdtfm", [32, 1024], BF16)
    wdtT = b.sb(f"{tag}_wdtT", [128, 8, 32], BF16)
    wdtok = b.dma("gpsimd", wdt_fm[:], W["in_proj_w"][4352:4384, :])
    wdt_last = None
    for kk in range(8):
        bank = 1 + (kk % 2)
        warj = ar.war([bank])
        psb = ar.bank_bf16(bank, rows=128, cols=32)
        trr = b.t(lambda t, kk=kk, psb=psb: t.transpose(
            psb, wdt_fm[:, kk * 128:(kk + 1) * 128], idb[0:32, 0:32]),
            waits=[wdtok, c.tok("ident128_bf16")] + warj, inc=True)
        wdt_last = b.v(lambda v, kk=kk, psb=psb: v.tensor_copy(
            wdtT[:, kk, :], psb), waits=[trr])
        ar.use([bank], wdt_last)

    S_run = b.sb(f"{tag}_Srun", [128, DIN_L1A], F32)
    S_bf = b.sb(f"{tag}_Sbf", [128, DIN_L1A], BF16)
    prev = {}
    prev["srun"] = b.v(lambda v: v.memset(S_run[:], 0.0))
    prev["sbf"] = b.v(lambda v: v.memset(S_bf[:], 0.0))

    ln = LN4(b, c, f"{tag}_ln", ar, 0, 1, 2, 3, sl=TS_L1A)
    xbf_ring = b.sb(f"{tag}_xbfr", [128, 2, ND, TS_L1A], BF16)
    qln_st = b.sb(f"{tag}_qlnst", [1, 2 * TS_L1A], F32)
    qln_bf = b.sb(f"{tag}_qlnbf", [1, 2 * TS_L1A], BF16)

    u_bf = sa.tile([128, ND, TS_L1A], BF16)
    wip_ring = sa.tile([128, 2, 8, 512], BF16)
    woutT = sa.tile([128, 16, 1024], BF16)
    z_fm = sa.tile([128, NZ_L1A, TS_L1A], BF16)
    xbce = sa.tile([128, NXB_L1A, 4 + TS_L1A], BF16)
    cacc = sa.tile([128, 2, TS_L1A], F32)
    xbcs = sa.tile([128, NXB_L1A, TS_L1A], BF16)
    xs_tm = sa.tile([128, DIN_L1A], BF16)
    zs_tm = sa.tile([128, NCH_L1A, DIN_L1A], BF16)
    xd_tm = sa.tile([128, DIN_L1A], BF16)
    xdd = sa.tile([128, DIN_L1A], BF16)
    y_tm = sa.tile([128, DIN_L1A], F32)
    ysc = sa.tile([128, DIN_L1A], F32)
    ynb = sa.tile([128, DIN_L1A], BF16)
    yfm = sa.tile([128, 16, TS_L1A], BF16)
    dt32 = sa.tile([32, TS_L1A], F32)
    scr32 = sa.tile([32, TS_L1A], F32)
    acs = sa.tile([32, TS_L1A], F32)
    acsT = sa.tile([128, NCH_L1A, 32], F32)
    dtT = sa.tile([128, NCH_L1A, 32], F32)
    eacs = sa.tile([128, 32], F32)
    e2 = sa.tile([128, 32], F32)
    dec32 = sa.tile([128, 32], F32)
    endrow = sa.tile([1, 32], F32)
    endrep = sa.tile([128, 2, 32], F32)
    M1sb = sa.tile([128, 16, CH_L1A], F32)
    LmG = sa.tile([128, 512], BF16)
    Gmb = sa.tile([128, CH_L1A], BF16)
    BTb = sa.tile([128, CH_L1A], BF16)
    tyb = sa.tile([128, 256], F32)
    x32b = sa.tile([128, ND, TS_L1A], F32)
    outb = sa.tile([128, ND, TS_L1A], F32)
    rstd = sa.tile([128, 1], F32)
    sqs = xdd

    wout_toks = []
    for isl in range(16):
        wout_toks.append(b.dma("sync", woutT[:, isl, :],
                               wop_bf[:, isl * 128:(isl + 1) * 128],
                               transpose=True, waits=[cast2] if isl == 0 else ()))
    wout_tok = wout_toks[-1]

    prev["hist"] = b.v(lambda v: v.memset(xbce[:, :, 0:4], 0.0))

    def do_segment(seg):
        t0 = seg * TS_L1A
        par = seg % 2
        if seg == 0:
            b.wait("vector", [nrm_tok])
        war = [prev[("xbf", par)]] if ("xbf", par) in prev else []
        xbf = xbf_ring[:, par, :, :]
        xtok = b.dma("gpsimd", xbf, x_r[:, :, t0:t0 + TS_L1A], waits=war)
        if "useg" in prev:
            b.wait("vector", [prev["useg"]])
        if "qlnuse" in prev:
            b.wait("vector", [prev["qlnuse"]])
        ln_tok = ln.slice(xbf, xtok, u_bf, eps1=1e-5, eps2=1e-5, wvec=nrm[:],
                          dual=(qln_st[:, 0:TS_L1A], qln_st[:, TS_L1A:2 * TS_L1A]))
        qcast = b.v(lambda v: v.tensor_copy(qln_bf[:], qln_st[:]),
                    waits=[ln_tok])
        prev["qlncast"] = qcast
        prev[("xbf", par)] = ln_tok

        if seg == 0:
            hist = prev["hist"]
        else:
            hist = b.v(lambda v: v.tensor_copy(xbce[:, :, 1:4],
                                               xbce[:, :, TS_L1A + 1:TS_L1A + 4]),
                       waits=[prev["hist"]])

        ping = [4, 5, 6]
        it = 0
        ztoks = {}
        xbctoks = {}
        jchunks = [(k * 512, 512) for k in range(8)] + [(4096, 256)]
        for ci, (j0, jn) in enumerate(jchunks):
            slot = ci % 2
            warw = [prev[("wip", slot)]] if ("wip", slot) in prev else []
            wtoks = []
            for dsl in range(8):
                wtoks.append(b.dma(
                    "sync", wip_ring[:, slot, dsl, 0:jn],
                    wip_bf[j0:j0 + jn, dsl * 128:(dsl + 1) * 128],
                    transpose=True,
                    waits=(warw + [cast1]) if dsl == 0 else ()))
            for jj in range(jn // 128):
                j = j0 + jj * 128
                bank = ping[it % 3]
                ps = ar.bank(bank, cols=TS_L1A)
                war2 = ar.war([bank])
                mmj = None
                for kk in range(ND):
                    mmj = b.t(lambda t, kk=kk, jj=jj, ps=ps, slot=slot: t.matmul(
                        ps, wip_ring[:, slot, kk, jj * 128:(jj + 1) * 128],
                        u_bf[:, kk, :], start=(kk == 0), stop=(kk == ND - 1)),
                        waits=([ln_tok] + wtoks + war2) if kk == 0 else (),
                        inc=(kk == ND - 1))
                prev[("wip", slot)] = mmj
                if j < DIN_L1A:
                    zi = j // 128
                    ev = b.s(lambda s, ps=ps, zi=zi: s.activation(
                        z_fm[:, zi, :], ps, AF.Silu), waits=[mmj])
                    ztoks[zi] = ev
                else:
                    xi = (j - DIN_L1A) // 128
                    ev = b.v(lambda v, ps=ps, xi=xi: v.tensor_copy(
                        xbce[:, xi, 4:4 + TS_L1A], ps), waits=[mmj, hist])
                    xbctoks[xi] = ev
                ar.use([bank], ev)
                it += 1
        bank = ping[it % 3]
        it += 1
        psd = ar.bank(bank, rows=32, cols=TS_L1A)
        war2 = ar.war([bank])
        mmd = None
        for kk in range(ND):
            mmd = b.t(lambda t, kk=kk, psd=psd: t.matmul(
                psd, wdtT[:, kk, :], u_bf[:, kk, :],
                start=(kk == 0), stop=(kk == ND - 1)),
                waits=([ln_tok, wdt_last] + war2) if kk == 0 else (),
                inc=(kk == ND - 1))
        dt_tok = b.v(lambda v: v.tensor_scalar_add(dt32[0:32, :], psd, dtb[:]),
                     waits=[mmd, dtb_tok])
        ar.use([bank], dt_tok)

        acc_t = {}
        for kt in range(NXB_L1A):
            eng = b.v
            ai = kt % 2
            wv = [xbctoks[kt], convw_tok] + ([acc_t[kt - 2]] if kt - 2 in acc_t
                                             else [])
            tok = eng(lambda e, kt=kt, ai=ai: e.tensor_scalar(
                cacc[:, ai, :], xbce[:, kt, 1:1 + TS_L1A], convw[:, kt, 0:1], None,
                ALU.mult), waits=wv)
            for tap in range(1, 4):
                tok = eng(lambda e, kt=kt, tap=tap, ai=ai: e.scalar_tensor_tensor(
                    cacc[:, ai, :], xbce[:, kt, 1 + tap:1 + tap + TS_L1A],
                    convw[:, kt, tap:tap + 1], cacc[:, ai, :], ALU.mult, ALU.add))
            sct = b.s(lambda s, kt=kt, ai=ai: s.activation(
                xbcs[:, kt, :], cacc[:, ai, :], AF.Silu,
                bias=convb[:, kt:kt + 1]), waits=[tok, convb_tok])
            acc_t[kt] = sct
        conv_done = acc_t[NXB_L1A - 1]
        conv_done2 = acc_t[NXB_L1A - 2]
        prev["hist"] = (b.chain["scalar"], b.chain["scalar"].count)

        b.s(lambda s: s.activation(scr32[0:32, :], dt32[0:32, :], AF.Abs),
            waits=[dt_tok])
        b.s(lambda s: s.activation(scr32[0:32, :], scr32[0:32, :], AF.Exp,
                                   scale=-1.0))
        lt = b.s(lambda s: s.activation(scr32[0:32, :], scr32[0:32, :], AF.Ln,
                                        bias=1.0))
        b.v(lambda v: v.tensor_scalar_max(dt32[0:32, :], dt32[0:32, :], 0.0),
            waits=[lt])
        dt_ok = b.v(lambda v: v.tensor_tensor(dt32[0:32, :], dt32[0:32, :],
                                              scr32[0:32, :], ALU.add))
        b.v(lambda v: v.tensor_scalar_mul(scr32[0:32, :], dt32[0:32, :],
                                          A_neg[:]), waits=[aneg_tok])
        acs_tok = b.v(lambda v: v.tensor_tensor_scan(
            acs[0:32, :], scr32[0:32, :], scr32[0:32, :], 0.0,
            ALU.add, ALU.bypass))
        war_acsd = [prev["acsd_use"]] if "acsd_use" in prev else []
        acsd = b.dma("sync", acs_dram[:], acs[0:32, :],
                     waits=[acs_tok] + war_acsd)

        tr_last = None
        for cc in range(NCH_L1A):
            bank = 4 + (cc % 2)
            psT = ar.bank(bank, cols=64)
            war3 = ar.war([bank])
            b.t(lambda t, cc=cc, psT=psT: t.transpose(
                psT[:, 0:32], acs[0:32, cc * CH_L1A:(cc + 1) * CH_L1A], idf[0:32, 0:32]),
                waits=[acs_tok, c.tok("ident128_f32")] + war3)
            tr2 = b.t(lambda t, cc=cc, psT=psT: t.transpose(
                psT[:, 32:64], dt32[0:32, cc * CH_L1A:(cc + 1) * CH_L1A],
                idf[0:32, 0:32]), waits=[dt_ok], inc=True)
            b.v(lambda v, cc=cc, psT=psT: v.tensor_copy(
                acsT[:, cc, :], psT[:, 0:32]), waits=[tr2])
            tr_last = b.v(lambda v, cc=cc, psT=psT: v.tensor_copy(
                dtT[:, cc, :], psT[:, 32:64]))
            ar.use([bank], tr_last)

        ztr_last = None
        for cc in range(NCH_L1A):
            for grp in range(4):
                bank = 4 + ((cc * 4 + grp) % 2)
                psb = ar.bank_bf16(bank, cols=512)
                war3 = ar.war([bank])
                tk = None
                for q in range(4):
                    zi = grp * 4 + q
                    tk = b.t(lambda t, cc=cc, zi=zi, q=q, psb=psb: t.transpose(
                        psb[:, q * 128:(q + 1) * 128],
                        z_fm[:, zi, cc * CH_L1A:(cc + 1) * CH_L1A], idb[:]),
                        waits=([ztoks[zi]] + war3) if q == 0 else [ztoks[zi]],
                        inc=(q == 3))
                ztr_last = b.v(lambda v, cc=cc, grp=grp, psb=psb: v.tensor_copy(
                    zs_tm[:, cc, grp * 512:(grp + 1) * 512], psb), waits=[tk])
                ar.use([bank], ztr_last)
        if dbg is not None:
            d1 = b.dma("sync", dbg["dbg_xbcs"].rearrange(
                "(kt p) t -> p kt t", p=128)[:, :, t0:t0 + TS_L1A], xbcs[:],
                waits=[conv_done])
            d2 = b.dma("sync", dbg["dbg_dt"][:, t0:t0 + TS_L1A], dt32[0:32, :],
                       waits=[acs_tok])
            b.wait("scalar", [d1, d2])
            b.wait("vector", [d1, d2])
        return dict(ln_tok=ln_tok, conv_done=conv_done, acs_tok=acs_tok,
                    acsd=acsd, ztr_last=ztr_last, par=par, t0=t0,
                    x32tok=None)

    def do_chunk(seg, cc, st):
        tc = cc * CH_L1A
        xtr_last = None
        for grp in range(4):
            bank = 4 + (grp % 2)
            psb = ar.bank_bf16(bank, cols=512)
            war3 = ar.war([bank])
            tk = None
            for q in range(4):
                xi = grp * 4 + q
                tk = b.t(lambda t, xi=xi, q=q, psb=psb, tc=tc: t.transpose(
                    psb[:, q * 128:(q + 1) * 128],
                    xbcs[:, xi, tc:tc + CH_L1A], idb[:]),
                    waits=([st["conv_done"]] + war3) if q == 0 else (),
                    inc=(q == 3))
            xtr_last = b.v(lambda v, grp=grp, psb=psb: v.tensor_copy(
                xs_tm[:, grp * 512:(grp + 1) * 512], psb), waits=[tk])
            ar.use([bank], xtr_last)
        xd_ok = b.v(lambda v, cc=cc: v.tensor_tensor(
            xd_tm[:].rearrange("p (h q) -> p h q", h=32),
            xs_tm[:].rearrange("p (h q) -> p h q", h=32),
            dtT[:, cc, :].unsqueeze(2).to_broadcast([128, 32, 64]), ALU.mult))
        psG = ar.bank(0, cols=CH_L1A)
        warG = ar.war([0])
        gmm = b.t(lambda t, tc=tc: t.matmul(
            psG, xbcs[:, 16, tc:tc + CH_L1A], xbcs[:, 17, tc:tc + CH_L1A],
            start=True, stop=True), waits=[st["conv_done"]] + warG, inc=True)
        gmask = b.v(lambda v: v.tensor_tensor(Gmb[:], psG, triuI[:], ALU.mult),
                    waits=[gmm, c.tok("triuI128")])
        ar.use([0], gmask)
        psBT = ar.bank_bf16(0, cols=128)
        bt_t = b.t(lambda t, tc=tc: t.transpose(
            psBT, xbcs[:, 16, tc:tc + CH_L1A], idb[:]), waits=[gmask], inc=True)
        btev = b.v(lambda v: v.tensor_copy(BTb[:], psBT), waits=[bt_t])
        ar.use([0], btev)
        ps4 = ar.bank(4, cols=96)
        war4 = ar.war([4])
        ext = b.t(lambda t, cc=cc: t.matmul(
            ps4[0:1, 0:32], sel_last_f[:], acsT[:, cc, :], start=True, stop=True),
            waits=[(b.chain["vector"], b.chain["vector"].count),
                   c.tok("sel_last_f32")] + war4, inc=True)
        erow = b.v(lambda v: v.tensor_copy(endrow[:], ps4[0:1, 0:32]),
                   waits=[ext])
        bcm = b.t(lambda t: t.matmul(ps4[:, 32:64], onesrow_f[:], endrow[:],
                                     start=True, stop=True),
                  waits=[erow, c.tok("ones_row_f32")], inc=True)
        erep = b.v(lambda v, cc=cc: v.tensor_copy(endrep[:, cc, :],
                                                  ps4[:, 32:64]), waits=[bcm])
        ar.use([4], erep)
        if cc == 0:
            ea = b.s(lambda s, cc=cc: s.activation(eacs[:], acsT[:, cc, :],
                                                   AF.Exp), waits=[erep])
            dc = b.s(lambda s, cc=cc: s.activation(dec32[:], endrep[:, cc, :],
                                                   AF.Exp))
        else:
            t_e = b.v(lambda v, cc=cc: v.tensor_tensor(
                eacs[:], acsT[:, cc, :], endrep[:, 1 - cc, :], ALU.subtract),
                waits=[erep])
            ea = b.s(lambda s: s.activation(eacs[:], eacs[:], AF.Exp),
                     waits=[t_e])
            t_d = b.v(lambda v, cc=cc: v.tensor_tensor(
                dec32[:], endrep[:, cc, :], endrep[:, 1 - cc, :], ALU.subtract))
            dc = b.s(lambda s: s.activation(dec32[:], dec32[:], AF.Exp),
                     waits=[t_d])
        t_2 = b.v(lambda v, cc=cc: v.tensor_tensor(
            e2[:], endrep[:, cc, :], acsT[:, cc, :], ALU.subtract))
        e2a = b.s(lambda s: s.activation(e2[:], e2[:], AF.Exp), waits=[t_2])
        xdd_ok = b.v(lambda v: v.tensor_tensor(
            xdd[:].rearrange("p (h q) -> p h q", h=32),
            xd_tm[:].rearrange("p (h q) -> p h q", h=32),
            e2[:].unsqueeze(2).to_broadcast([128, 32, 64]), ALU.mult),
            waits=[e2a])
        m1toks = {}

        def m1_load(half):
            if half in m1toks:
                return m1toks[half]
            war_m1 = [prev["m1use"]] if "m1use" in prev else []
            tokm = b.dma("gpsimd", M1sb[:],
                         acs_dram[half * 16:(half + 1) * 16,
                                  tc:tc + CH_L1A].unsqueeze(0).to_broadcast(
                             [128, 16, CH_L1A]), waits=[st["acsd"]] + war_m1)
            m1toks[half] = tokm
            prev["acsd_use"] = tokm
            return tokm
        y_last = None
        for hg in range(8):
            half = hg // 4
            if hg % 4 == 0:
                m1toks.pop(1 - half, None)
                if hg == 4:
                    prev["m1use"] = (b.chain["vector"], b.chain["vector"].count)
                m1toks.clear()
            m1tok = m1_load(half)
            hsl = slice(hg * 4, hg * 4 + 4)
            hloc = slice((hg % 4) * 4, (hg % 4) * 4 + 4)
            csl = slice(hg * 256, (hg + 1) * 256)
            m2v = acsT[:, cc, hsl].unsqueeze(2).to_broadcast([128, 4, CH_L1A])
            dv = b.v(lambda v, hloc=hloc, m2v=m2v: v.tensor_tensor(
                LmG[:].rearrange("p (h l) -> p h l", h=4), M1sb[:, hloc, :],
                m2v, ALU.subtract), waits=[m1tok])
            b.v(lambda v: v.tensor_scalar_min(LmG[:], LmG[:], 0.0))
            exx = b.s(lambda s: s.activation(LmG[:], LmG[:], AF.Exp),
                      waits=[(b.chain["vector"], b.chain["vector"].count)])
            gv = Gmb[:].unsqueeze(1).to_broadcast([128, 4, CH_L1A])
            lmok = b.v(lambda v, gv=gv: v.tensor_tensor(
                LmG[:].rearrange("p (h l) -> p h l", h=4),
                LmG[:].rearrange("p (h l) -> p h l", h=4), gv, ALU.mult),
                waits=[exx])
            psYd = ar.bank(2, cols=256)
            psYo = ar.bank(3, cols=256)
            warY = ar.war([2])
            warYo = ar.war([3])
            ydm = None
            for q in range(4):
                h = hg * 4 + q
                ydm = b.t(lambda t, q=q, h=h: t.matmul(
                    psYd[:, q * 64:(q + 1) * 64], LmG[:, q * CH_L1A:(q + 1) * CH_L1A],
                    xd_tm[:, h * 64:(h + 1) * 64], start=True, stop=True),
                    waits=([lmok, xd_ok] + warY) if q == 0 else (),
                    inc=(q == 3))
            yom = b.t(lambda t, tc=tc, csl=csl: t.matmul(
                psYo, xbcs[:, 17, tc:tc + CH_L1A], S_bf[:, csl],
                start=True, stop=True),
                waits=[prev["sbf"], st["conv_done"]] + warYo, inc=True)
            eav = eacs[:, hsl].unsqueeze(2).to_broadcast([128, 4, 64])
            ty = b.v(lambda v, eav=eav: v.tensor_tensor(
                tyb[:].rearrange("p (h q) -> p h q", h=4), psYo.rearrange(
                    "p (h q) -> p h q", h=4), eav, ALU.mult),
                waits=[yom, ea])
            ar.use([3], ty)
            y_last = b.v(lambda v, csl=csl: v.tensor_tensor(
                y_tm[:, csl], tyb[:], psYd, ALU.add), waits=[ydm])
            ar.use([2], y_last)
        sup_last = None
        smm_last = None
        for g4 in range(4):
            gsl = slice(g4 * 512, (g4 + 1) * 512)
            psSt = ar.bank(7)
            war7 = ar.war([7])
            smm = b.t(lambda t, gsl=gsl: t.matmul(
                psSt, BTb[:], xdd[:, gsl], start=True, stop=True),
                waits=([btev, xdd_ok] + war7) if g4 == 0 else war7, inc=True)
            smm_last = smm
            dcv = dec32[:, g4 * 8:(g4 + 1) * 8].unsqueeze(2).to_broadcast(
                [128, 8, 64])
            b.v(lambda v, gsl=gsl, dcv=dcv: v.tensor_tensor(
                S_run[:, gsl].rearrange("p (h q) -> p h q", h=8),
                S_run[:, gsl].rearrange("p (h q) -> p h q", h=8), dcv, ALU.mult),
                waits=[dc, smm])
            sup_last = b.v(lambda v, gsl=gsl, psSt=psSt: v.tensor_tensor(
                S_run[:, gsl], S_run[:, gsl], psSt, ALU.add))
            ar.use([7], sup_last)
        prev["sbf"] = b.v(lambda v: v.tensor_copy(S_bf[:], S_run[:]),
                          waits=[sup_last])
        prev["m1use"] = (b.chain["vector"], b.chain["vector"].count)
        m1toks.clear()
        if dbg is not None:
            ci = (st["t0"] // TS_L1A) * NCH_L1A + cc
            dy = b.dma("sync", dbg["dbg_y"][ci], y_tm[:], waits=[y_last])
            b.wait("vector", [dy])
        b.v(lambda v: v.tensor_tensor(
            ysc[:].rearrange("p (h q) -> p h q", h=32),
            xs_tm[:].rearrange("p (h q) -> p h q", h=32),
            D_rep[:].unsqueeze(2).to_broadcast([128, 32, 64]), ALU.mult),
            waits=[drep_tok, y_last])
        b.v(lambda v: v.tensor_tensor(ysc[:], ysc[:], y_tm[:], ALU.add))
        gat = b.v(lambda v, cc=cc: v.tensor_tensor(
            ysc[:], ysc[:], zs_tm[:, cc, :], ALU.mult), waits=[st["ztr_last"]])
        sq = b.s(lambda s: s.activation(sqs[:], ysc[:], AF.Square),
                 waits=[gat, smm_last])
        red = b.v(lambda v: v.reduce_sum(rstd[:], sqs[:], axis=AX.X), waits=[sq])
        b.v(lambda v: v.tensor_scalar(rstd[:], rstd[:], 1.0 / DIN_L1A, 1e-5,
                                      ALU.mult, ALU.add))
        lnr = b.s(lambda s: s.activation(rstd[:], rstd[:], AF.Ln),
                  waits=[(b.chain["vector"], b.chain["vector"].count)])
        er = b.s(lambda s: s.activation(rstd[:], rstd[:], AF.Exp, scale=-0.5))
        b.v(lambda v: v.tensor_scalar_mul(ynb[:], ysc[:], rstd[:]), waits=[er])
        yn_ok = b.v(lambda v: v.tensor_tensor(ynb[:], ynb[:], gnw_rep[:],
                                              ALU.mult), waits=[gn_last])
        ytr_last = None
        for grp in range(4):
            bank = 5 + (grp % 2)
            psb = ar.bank_bf16(bank, cols=512)
            war3 = ar.war([bank])
            tk = None
            for q in range(4):
                yi = grp * 4 + q
                tk = b.t(lambda t, yi=yi, q=q, psb=psb: t.transpose(
                    psb[:, q * 128:(q + 1) * 128],
                    ynb[:, yi * 128:(yi + 1) * 128], idb[:]),
                    waits=([yn_ok] + war3) if q == 0 else (), inc=(q == 3))
            ytr_last = b.v(lambda v, grp=grp, psb=psb, tc=tc: v.tensor_copy(
                yfm[:, grp * 4:(grp + 1) * 4, tc:tc + CH_L1A],
                psb.rearrange("p (a l) -> p a l", a=4)), waits=[tk])
            ar.use([bank], ytr_last)
        return ytr_last

    def out_proj_segment(seg, st, ytoks):
        t0 = st["t0"]
        par = st["par"]
        war_x = [prev["xres"]] if "xres" in prev else []
        x32tok = b.dma("sync", x32b, x_r[:, :, t0:t0 + TS_L1A], waits=war_x)
        war4 = ar.war([4])
        b.t(lambda t: t.matmul(ar.bank(4, cols=TS_L1A), onesrow[:],
                               qln_bf[:, 0:TS_L1A], start=True, stop=True),
            waits=[prev["qlncast"], c.tok("ones_row_bf16")] + war4)
        bcm = b.t(lambda t: t.matmul(ar.bank(4, cols=512)[:, 256:256 + TS_L1A],
                                     onesrow[:], qln_bf[:, TS_L1A:2 * TS_L1A],
                                     start=True, stop=True), inc=True)
        prev["qlnuse"] = (b.chain["tensor"], b.chain["tensor"].count)
        res_last = None
        war_ob = [prev["outdma"]] if "outdma" in prev else []
        for dtile in range(ND):
            bank = 5 + (dtile % 2)
            ps = ar.bank(bank, cols=TS_L1A)
            war2 = ar.war([bank])
            mm = None
            for kk in range(16):
                mm = b.t(lambda t, kk=kk, dtile=dtile, ps=ps: t.matmul(
                    ps, woutT[:, kk, dtile * 128:(dtile + 1) * 128],
                    yfm[:, kk, :], start=(kk == 0), stop=(kk == 15)),
                    waits=([ytoks] + wout_toks + war2) if kk == 0 else (),
                    inc=(kk == 15))
            b.v(lambda v, dtile=dtile: v.tensor_tensor(
                outb[:, dtile, :], x32b[:, dtile, :],
                ar.bank(4, cols=TS_L1A), ALU.mult),
                waits=([bcm, x32tok] + war_ob) if dtile == 0 else ())
            b.v(lambda v, dtile=dtile: v.tensor_tensor(
                outb[:, dtile, :], outb[:, dtile, :],
                ar.bank(4, cols=512)[:, 256:256 + TS_L1A], ALU.add))
            b.v(lambda v, dtile=dtile: v.tensor_tensor(
                outb[:, dtile, :], outb[:, dtile, :], x32b[:, dtile, :],
                ALU.add))
            res_last = b.v(lambda v, dtile=dtile, ps=ps: v.tensor_tensor(
                outb[:, dtile, :], outb[:, dtile, :], ps, ALU.add),
                waits=[mm])
            ar.use([bank], res_last)
        ar.use([4], res_last)
        prev["xres"] = res_last
        otok = b.dma("sync", o_r[:, :, t0:t0 + TS_L1A], outb, waits=[res_last])
        prev["outdma"] = otok
        prev["useg"] = mm
        return otok

    otok = None
    for seg in range(NSEG_L1A):
        st = do_segment(seg)
        yt = None
        for cc in range(NCH_L1A):
            yt = do_chunk(seg, cc, st)
        otok = out_proj_segment(seg, st, yt)
    return otok


NE_L1F = 8
NI_L1F = 4
NSH_L1F = 8
SLF_L1F = 256
NSLF_L1F = T // SLF_L1F


def emit_l1ffn(b: KB, c: Consts, ar: Arena, x_fm, out_fm, W, tag="l1f"):
    x_r = x_fm.rearrange("(dt p) t -> p dt t", p=128)
    o_r = out_fm.rearrange("(dt p) t -> p dt t", p=128)

    def ldw(nm, ap):
        k_dim, m_dim = ap.shape
        nk = k_dim // 128
        tile = b.sb(f"{tag}_{nm}", [128, nk, m_dim], BF16)
        tok = b.dma("gpsimd", tile[:], ap.rearrange("(k p) m -> p k m", p=128))
        return tile, tok

    gw, gw_tok = ldw("gw", W["gate_w"])
    swg, swg_tok = ldw("swg", W["sw_gate"])
    swu, swu_tok = ldw("swu", W["sw_up"])
    swd, swd_tok = ldw("swd", W["sw_down"])
    gbias = b.sb(f"{tag}_gbias", [8, 1], F32)
    gb_tok = b.dma("sync", gbias[:], W["gate_bias"].unsqueeze(1))

    wg_ring = b.sb(f"{tag}_wgr", [128, 2, 8, 512], BF16)
    wu_ring = b.sb(f"{tag}_wur", [128, 2, 8, 512], BF16)
    wd_ring = b.sb(f"{tag}_wdr", [128, 2, 4, 1024], BF16)
    ring_war = {}
    wd_war = {}
    w_toks = {}

    def prefetch(s, e):
        if (s, e) in w_toks or not (0 <= s < NSLF_L1F):
            return
        slot = (s * NE_L1F + e) % 2
        war_gu = [ring_war[slot]] if slot in ring_war else []
        war_d = [wd_war[slot]] if slot in wd_war else []
        tg = b.dma("gpsimd", wg_ring[:, slot, :, :],
                   W["w_gate"][e].rearrange("(k p) m -> p k m", p=128), waits=war_gu)
        tu = b.dma("gpsimd", wu_ring[:, slot, :, :],
                   W["w_up"][e].rearrange("(k p) m -> p k m", p=128))
        td = b.dma("gpsimd", wd_ring[:, slot, :, :],
                   W["w_down"][e].rearrange("(k p) m -> p k m", p=128), waits=war_d)
        w_toks[(s, e)] = (tg, tu, td, slot)

    ln = LN4(b, c, f"{tag}_ln", ar, 4, 5, 6, 7, sl=SLF_L1F)
    xbf_ring = b.sb(f"{tag}_xbfr", [128, 2, ND, SLF_L1F], BF16)
    u_ring = b.sb(f"{tag}_ur", [128, 2, ND, SLF_L1F], BF16)
    x32_ring = b.sb(f"{tag}_x32r", [128, 2, ND, SLF_L1F], F32)
    out_ring = b.sb(f"{tag}_outr", [128, 2, ND, SLF_L1F], F32)
    sfc = b.sb(f"{tag}_sfc", [8, SLF_L1F], F32)
    NB = SLF_L1F // 128
    sfcT = b.sb(f"{tag}_sfcT", [128, NB, 8], F32)
    gsb = b.sb(f"{tag}_gs", [128, NB, 4], F32)
    m1b = b.sb(f"{tag}_m1", [128, NB], F32)
    m2b = b.sb(f"{tag}_m2", [128, NB], F32)
    v1b = b.sb(f"{tag}_v1", [128, NB], F32)
    v2b = b.sb(f"{tag}_v2", [128, NB], F32)
    recb = b.sb(f"{tag}_rec", [128, NB], F32)
    eqb = b.sb(f"{tag}_eq", [128, NB, 8], F32)
    mkb = b.sb(f"{tag}_mk", [128, NB, 8], F32)
    tmpb = b.sb(f"{tag}_tmp", [128, NB, 8], F32)
    wfullb = b.sb(f"{tag}_wfull", [128, NB, 8], F32)
    wfT = b.sb(f"{tag}_wfT", [8, SLF_L1F], BF16)
    wrep = b.sb(f"{tag}_wrep", [128, NE_L1F, SLF_L1F], BF16)
    gbf = b.sb(f"{tag}_gbf", [128, 2, SLF_L1F], BF16)
    h_all = b.sb(f"{tag}_hall", [128, NE_L1F, NI_L1F, SLF_L1F], BF16)
    prod_sh = b.sb(f"{tag}_prsh", [128, NSH_L1F, SLF_L1F], BF16)

    prev = {}
    idf = c["ident128_f32"]
    sel8 = c["sel8_bf16"]
    gbf_war = {}
    dn_tok_prev = None
    otok = None

    prefetch(0, 0)
    prefetch(0, 1)

    def do_slice(s):
        nonlocal dn_tok_prev, otok
        sl = slice(s * SLF_L1F, (s + 1) * SLF_L1F)
        par = s % 2
        war_x = [prev[("xbf", par)]] if ("xbf", par) in prev else []
        xbf = xbf_ring[:, par, :, :]
        xtok = b.dma("gpsimd", xbf, x_r[:, :, sl], waits=war_x)
        war_x32 = [prev[("x32", par)]] if ("x32", par) in prev else []
        x32 = x32_ring[:, par, :, :]
        x32tok = b.dma("sync", x32, x_r[:, :, sl], waits=war_x32)

        if dn_tok_prev is not None:
            b.wait("vector", [dn_tok_prev])

        u = u_ring[:, par, :, :]
        ln_tok = ln.slice(xbf, xtok, u)
        prev[("xbf", par)] = ln_tok

        lg_ps = ar.bank(4, rows=8, cols=SLF_L1F)
        war_lg = ar.war([4])
        lg_tok = None
        for k in range(ND):
            lg_tok = b.t(lambda t, k=k: t.matmul(lg_ps, gw[:, k, :], u[:, k, :],
                                                 start=(k == 0), stop=(k == ND - 1)),
                         waits=([ln_tok, gw_tok] + war_lg) if k == 0 else (),
                         inc=(k == ND - 1))
        stok = b.s(lambda s_: s_.activation(sfc[:], lg_ps, AF.Sigmoid),
                   waits=[lg_tok, gb_tok])
        bias_tok = b.v(lambda v: v.tensor_scalar_add(sfc[:], sfc[:], gbias[:]),
                       waits=[stok])
        ar.use([4], bias_tok)
        tr_ps = ar.bank(5, rows=128, cols=8 * NB)
        war_tr = ar.war([5])
        tr_tok = None
        for j in range(NB):
            tr_tok = b.t(lambda t, j=j: t.transpose(
                tr_ps[:, j * 8:(j + 1) * 8], sfc[:, j * 128:(j + 1) * 128],
                idf[0:8, 0:8]),
                waits=([bias_tok, c.tok("ident128_f32")] + war_tr) if j == 0 else (),
                inc=(j == NB - 1))
        cp_tok = b.v(lambda v: v.tensor_copy(
            sfcT[:].rearrange("p a e -> p (a e)"), tr_ps[:, 0:8 * NB]),
            waits=[tr_tok])
        ar.use([5], cp_tok)
        sfcT4 = sfcT[:].rearrange("p a (g two) -> p a g two", two=2)
        b.v(lambda v: v.reduce_sum(gsb[:], sfcT4, axis=AX.X))
        b.v(lambda v: v.reduce_max(m1b[:], gsb[:], axis=AX.X))
        m1x = m1b[:].unsqueeze(2).to_broadcast([128, NB, 4])
        b.v(lambda v: v.tensor_tensor(mkb[:, :, 0:4], gsb[:], m1x, ALU.is_equal))
        b.v(lambda v: v.scalar_tensor_tensor(
            mkb[:, :, 4:8], mkb[:, :, 0:4], -1e30, gsb[:], ALU.mult, ALU.add))
        b.v(lambda v: v.reduce_max(m2b[:], mkb[:, :, 4:8], axis=AX.X))
        m2x = m2b[:].unsqueeze(2).to_broadcast([128, NB, 4])
        b.v(lambda v: v.tensor_tensor(mkb[:, :, 0:4], gsb[:], m2x, ALU.is_ge))
        gm4 = mkb[:, :, 0:4].unsqueeze(3).to_broadcast([128, NB, 4, 2])
        b.v(lambda v: v.tensor_tensor(
            tmpb[:].rearrange("p a (g two) -> p a g two", two=2), sfcT4, gm4,
            ALU.mult))
        b.v(lambda v: v.reduce_max(v1b[:], tmpb[:], axis=AX.X))
        v1x = v1b[:].unsqueeze(2).to_broadcast([128, NB, 8])
        b.v(lambda v: v.tensor_tensor(eqb[:], tmpb[:], v1x, ALU.is_equal))
        b.v(lambda v: v.scalar_tensor_tensor(
            eqb[:], eqb[:], -1e30, tmpb[:], ALU.mult, ALU.add))
        b.v(lambda v: v.reduce_max(v2b[:], eqb[:], axis=AX.X))
        b.v(lambda v: v.tensor_tensor(recb[:], v1b[:], v2b[:], ALU.add))
        b.v(lambda v: v.tensor_scalar_add(recb[:], recb[:], 1e-20))
        b.v(lambda v: v.reciprocal(recb[:], recb[:]))
        b.v(lambda v: v.tensor_scalar_mul(recb[:], recb[:], 2.5))
        v2x = v2b[:].unsqueeze(2).to_broadcast([128, NB, 8])
        b.v(lambda v: v.tensor_tensor(eqb[:], tmpb[:], v2x, ALU.is_ge))
        b.v(lambda v: v.tensor_tensor(wfullb[:], eqb[:], tmpb[:], ALU.mult))
        recx = recb[:].unsqueeze(2).to_broadcast([128, NB, 8])
        wf_tok = b.v(lambda v: v.tensor_tensor(wfullb[:], wfullb[:], recx, ALU.mult))
        trb_ps = ar.bank(5, rows=8, cols=SLF_L1F)
        war_tr2 = ar.war([5])
        tr2 = None
        for j in range(NB):
            tr2 = b.t(lambda t, j=j: t.transpose(
                trb_ps[:, j * 128:(j + 1) * 128], wfullb[:, j, :], idf[:]),
                waits=([wf_tok] + war_tr2) if j == 0 else (), inc=(j == NB - 1))
        wfT_tok = b.v(lambda v: v.tensor_copy(wfT[:], trb_ps[:, 0:SLF_L1F]),
                      waits=[tr2])
        ar.use([5], wfT_tok)
        wr_ps = ar.bank(6, rows=128, cols=SLF_L1F)
        war_wr = ar.war([6])
        wr_evac = None
        for e in range(NE_L1F):
            wwaits = ([wfT_tok, c.tok("sel8_bf16")] + war_wr) if e == 0 else [wr_evac]
            wr_tok = b.t(lambda t, e=e: t.matmul(
                wr_ps, sel8[:, e * 128:(e + 1) * 128], wfT[:], start=True, stop=True),
                waits=wwaits, inc=True)
            wr_evac = b.v(lambda v, e=e: v.tensor_copy(wrep[:, e, :], wr_ps),
                          waits=[wr_tok])
        ar.use([6], wr_evac)

        y_ps = ar.span(0, 4)
        war_y = ar.war([0, 1, 2, 3])
        first_in_bank = [True] * 4
        it = 0
        dn_tok = None

        def gu_h(lhs_g, lhs_u, lhs_toks, dest, extra_mult=None):
            nonlocal it, dn_tok
            gp = 4 + (it % 2)
            up = 6 + (it % 2)
            gb_ps = ar.bank(gp, rows=128, cols=SLF_L1F)
            ub_ps = ar.bank(up, rows=128, cols=SLF_L1F)
            war_g = ar.war([gp])
            war_u2 = ar.war([up])
            gmm = None
            for k in range(ND):
                gmm = b.t(lambda t, k=k, gb_ps=gb_ps, lhs_g=lhs_g: t.matmul(
                    gb_ps, lhs_g(k), u[:, k, :], start=(k == 0), stop=(k == ND - 1)),
                    waits=(list(lhs_toks) + [ln_tok] + war_g) if k == 0 else (),
                    inc=(k == ND - 1))
            umm = None
            for k in range(ND):
                umm = b.t(lambda t, k=k, ub_ps=ub_ps, lhs_u=lhs_u: t.matmul(
                    ub_ps, lhs_u(k), u[:, k, :], start=(k == 0), stop=(k == ND - 1)),
                    waits=war_u2 if k == 0 else (), inc=(k == ND - 1))
            gpar = it % 2
            war_gbf = [gbf_war[gpar]] if gpar in gbf_war else []
            sg = b.s(lambda s_, gb_ps=gb_ps, gpar=gpar: s_.activation(
                gbf[:, gpar, :], gb_ps, AF.Silu), waits=[gmm] + war_gbf)
            t1 = b.v(lambda v, ub_ps=ub_ps, gpar=gpar, dest=dest: v.tensor_tensor(
                dest, gbf[:, gpar, :], ub_ps, ALU.mult), waits=[sg, umm])
            gbf_war[gpar] = t1
            if extra_mult is not None:
                t1 = b.v(lambda v, dest=dest, extra_mult=extra_mult: v.tensor_tensor(
                    dest, dest, extra_mult, ALU.mult))
            ar.use([gp], t1)
            ar.use([up], t1)
            it += 1
            return t1

        for e in range(NE_L1F):
            prefetch(s, e)
            tg, tu, td, slot = w_toks[(s, e)]
            h_tok = None
            for i in range(NI_L1F):
                h_tok = gu_h(
                    lambda k, i=i, slot=slot: wg_ring[:, slot, k, i * 128:(i + 1) * 128],
                    lambda k, i=i, slot=slot: wu_ring[:, slot, k, i * 128:(i + 1) * 128],
                    [tg, tu] if i == 0 else [],
                    h_all[:, e, i, :], extra_mult=wrep[:, e, :])
            last_gu = None
            for d in range(ND):
                bank_i = d // 2
                for k in range(NI_L1F):
                    st = first_in_bank[bank_i]
                    first_in_bank[bank_i] = False
                    w0 = ([h_tok, td] + war_y) if (d == 0 and k == 0) else ()
                    dn_tok = b.t(lambda t, d=d, e=e, k=k, slot=slot, st=st: t.matmul(
                        y_ps[:, d * 256:(d + 1) * 256],
                        wd_ring[:, slot, k, d * 128:(d + 1) * 128],
                        h_all[:, e, k, :], start=st, stop=False,
                        skip_group_check=True),
                        waits=w0, inc=(d == ND - 1 and k == NI_L1F - 1))
            ring_war[slot] = dn_tok
            wd_war[slot] = dn_tok
            if e + 2 < NE_L1F:
                prefetch(s, e + 2)
            else:
                prefetch(s + 1, e + 2 - NE_L1F)
        for i in range(NSH_L1F):
            t_sh = gu_h(
                lambda k, i=i: swg[:, k, i * 128:(i + 1) * 128],
                lambda k, i=i: swu[:, k, i * 128:(i + 1) * 128],
                [swg_tok, swu_tok] if i == 0 else [],
                prod_sh[:, i, :])
        for d in range(ND):
            for k in range(NSH_L1F):
                dn_tok = b.t(lambda t, d=d, k=k: t.matmul(
                    y_ps[:, d * 256:(d + 1) * 256],
                    swd[:, k, d * 128:(d + 1) * 128],
                    prod_sh[:, k, :], start=False, stop=(k == NSH_L1F - 1),
                    skip_group_check=True),
                    waits=[t_sh, swd_tok] if (d == 0 and k == 0) else (),
                    inc=(d == ND - 1 and k == NSH_L1F - 1))
        res_tok = None
        war_out = [prev[("out", par)]] if ("out", par) in prev else []
        for d in range(ND):
            res_tok = b.v(lambda v, d=d, x32=x32, par=par: v.tensor_tensor(
                out_ring[:, par, d, :], x32[:, d, :],
                y_ps[:, d * 256:(d + 1) * 256], ALU.add),
                waits=([dn_tok, x32tok] + war_out) if d == 0 else ())
        ar.use([0, 1, 2, 3], res_tok)
        dn_tok_prev = dn_tok
        prev[("x32", par)] = res_tok
        otok = b.dma("sync", o_r[:, :, sl], out_ring[:, par, :, :],
                     waits=[res_tok] + ([prev[("out", par)]] if ("out", par) in prev else []))
        prev[("out", par)] = otok

    for s in range(NSLF_L1F):
        do_slice(s)
    return otok



def _consts_caps(b, cn):
    caps = {k: b.dram(f"c_{k}", list(v.shape),
                      BF16 if v.dtype.name == "bfloat16" else F32,
                      kind="ExternalInput") for k, v in cn.items()}
    return Consts(b, caps)


def _build_l0attn(cn):
    nc = bass.Bass("TRN2", target_bir_lowering=False, debug=False)
    b = KB(nc)
    x_fm = b.dram("x_fm", [1024, 4096], F32, kind="ExternalInput")
    out_fm = b.dram("out_fm", [1024, 4096], F32, kind="ExternalOutput")
    W = {k: b.dram(k, list(shp), F32, kind="ExternalInput") for k, shp in (
        ("Wq", (1024, 1024)), ("Wk", (1024, 1024)), ("Wv", (1024, 1024)),
        ("Wb", (1024, 16)), ("Wo", (1024, 1024)), ("ln1", (1024,)),
        ("ln2", (1024,)), ("Wg", (1024, 4096)), ("Wu", (1024, 4096)),
        ("Wd", (4096, 1024)))}
    c = _consts_caps(b, cn)
    ar = Arena(b)
    sa = SbArena(b, 160 * 1024)
    emit_l0attn(b, c, ar, sa, x_fm, out_fm, W, dbg=None)
    b.barrier()
    b.finalize()
    return nc


def _build_l0ffn(cn):
    nc = bass.Bass("TRN2", target_bir_lowering=False, debug=False)
    b = KB(nc)
    x_fm = b.dram("x_fm", [1024, 4096], F32, kind="ExternalInput")
    out_fm = b.dram("out_fm", [1024, 4096], F32, kind="ExternalOutput")
    W = {k: b.dram(k, list(shp), F32, kind="ExternalInput") for k, shp in (
        ("gate_w", (16, 1024)), ("Wg", (16, 2048, 1024)),
        ("Wu", (16, 2048, 1024)), ("Wd", (16, 1024, 2048)),
        ("Ws_g", (2048, 1024)), ("Ws_u", (2048, 1024)),
        ("Ws_d", (1024, 2048)))}
    c = _consts_caps(b, cn)
    ar = Arena(b)
    sa = SbArena(b, 185 * 1024)
    emit_l0ffn(b, c, ar, sa, x_fm, out_fm, W, dbg=None)
    b.barrier()
    b.finalize()
    return nc


def _build_l1attn(cn):
    nc = bass.Bass("TRN2", target_bir_lowering=False, debug=False)
    b = KB(nc)
    x_fm = b.dram("x_fm", [1024, 4096], F32, kind="ExternalInput")
    out_fm = b.dram("out_fm", [1024, 4096], F32, kind="ExternalOutput")
    W = {k: b.dram(k, list(shp), F32, kind="ExternalInput") for k, shp in (
        ("norm_w", (1024,)), ("in_proj_w", (4384, 1024)),
        ("conv_w", (2304, 1, 4)), ("conv_b", (2304,)), ("dt_bias", (32,)),
        ("A_log", (32,)), ("D", (32,)), ("gnorm_w", (2048,)),
        ("out_proj_w", (1024, 2048)))}
    c = _consts_caps(b, cn)
    ar = Arena(b)
    sa = SbArena(b, 160 * 1024)
    emit_l1attn(b, c, ar, sa, x_fm, out_fm, W, dbg=None)
    b.barrier()
    b.finalize()
    return nc


def _build_l1ffn(cn):
    nc = bass.Bass("TRN2", target_bir_lowering=False, debug=False)
    b = KB(nc)
    x_fm = b.dram("x_fm", [1024, 4096], F32, kind="ExternalInput")
    out_fm = b.dram("out_fm", [1024, 4096], F32, kind="ExternalOutput")
    W = {
        "gate_w": b.dram("gate_w", [1024, 8], F32, kind="ExternalInput"),
        "gate_bias": b.dram("gate_bias", [8], F32, kind="ExternalInput"),
        "w_gate": b.dram("w_gate", [8, 1024, 512], F32, kind="ExternalInput"),
        "w_up": b.dram("w_up", [8, 1024, 512], F32, kind="ExternalInput"),
        "w_down": b.dram("w_down", [8, 512, 1024], F32, kind="ExternalInput"),
        "sw_gate": b.dram("sw_gate", [1024, 1024], F32, kind="ExternalInput"),
        "sw_up": b.dram("sw_up", [1024, 1024], F32, kind="ExternalInput"),
        "sw_down": b.dram("sw_down", [1024, 1024], F32, kind="ExternalInput"),
    }
    c = _consts_caps(b, cn)
    ar = Arena(b)
    emit_l1ffn(b, c, ar, x_fm, out_fm, W)
    b.barrier()
    b.finalize()
    return nc


_L0A_CONSTS = ("ident128_bf16", "ident64_bf16", "ones_col_bf16",
               "ones_row_bf16", "selhalf16", "headmaskT", "dup64", "maskLx2",
               "maskUSx2", "maskUIx2", "masklo", "maskhi", "ident128_f32")
_L0F_CONSTS = ("ident128_bf16", "ident128_f32", "ones_col_bf16",
               "ones_row_bf16", "ones_row_f32", "grp8_f32", "grp8T_f32")
_L1A_CONSTS = ("ident128_bf16", "ident128_f32", "ones_col_bf16",
               "ones_row_bf16", "ones_row_f32", "sel_last_f32", "triuI128")

_WMAP_L0A = {"Wq": "L0_attn_Wq", "Wk": "L0_attn_Wk", "Wv": "L0_attn_Wv",
             "Wb": "L0_attn_Wb", "Wo": "L0_attn_Wo", "ln1": "L0_attn_ln1",
             "ln2": "L0_attn_ln2", "Wg": "L0_attn_Wg", "Wu": "L0_attn_Wu",
             "Wd": "L0_attn_Wd"}
_WMAP_L0F = {k: f"L0_ffn_{k}" for k in
             ("gate_w", "Wg", "Wu", "Wd", "Ws_g", "Ws_u", "Ws_d")}
_WMAP_L1A = {k: f"L1_attn_{k}" for k in
             ("norm_w", "in_proj_w", "conv_w", "conv_b", "dt_bias", "A_log",
              "D", "gnorm_w", "out_proj_w")}
_WMAP_L1F = {k: f"L1_ffn_{k}" for k in
             ("gate_w", "gate_bias", "w_gate", "w_up", "w_down", "sw_gate",
              "sw_up", "sw_down")}


def _launch(nc, xs, wmap, cn, inputs):
    in_maps = []
    for i in range(8):
        m = {"x_fm": xs[i]}
        for k, src in wmap.items():
            m[k] = inputs[src]
        for k, v in cn.items():
            m[f"c_{k}"] = v
        in_maps.append(m)
    r = run_bass_kernel_spmd(nc, in_maps, core_ids=list(range(8)))
    return [np.ascontiguousarray(np.asarray(r.results[i]["out_fm"],
                                            dtype=np.float32))
            for i in range(8)]


def kernel(**inputs):
    x = np.asarray(inputs["L0_attn_hidden_states"], dtype=np.float32)
    cn_all = make_consts()
    xs = [np.ascontiguousarray(x[i].T) for i in range(8)]

    cn = {k: cn_all[k] for k in _L0A_CONSTS}
    xs = _launch(_build_l0attn(cn), xs, _WMAP_L0A, cn, inputs)

    cn = {k: cn_all[k] for k in _L0F_CONSTS}
    xs = _launch(_build_l0ffn(cn), xs, _WMAP_L0F, cn, inputs)

    cn = {k: cn_all[k] for k in _L1A_CONSTS}
    xs = _launch(_build_l1attn(cn), xs, _WMAP_L1A, cn, inputs)

    cn = dict(cn_all)
    xs = _launch(_build_l1ffn(cn), xs, _WMAP_L1F, cn, inputs)

    return np.stack([xs[i].T for i in range(8)]).astype(np.float32)
